# version 02
# baseline (speedup 1.0000x reference)
import numpy as np
from contextlib import ExitStack

import concourse.bass as bass
import concourse.mybir as mybir
from concourse.bass_utils import run_bass_kernel_spmd

F32 = mybir.dt.float32
BF16 = mybir.dt.bfloat16
U8 = mybir.dt.uint8
AF = mybir.ActivationFunctionType
ALU = mybir.AluOpType
AX = mybir.AxisListType

D = 1024
CH = D // 128
T = 2048
NTR = T // 512
NTT = T // 128
H = 16
DH = 64
FF = 4096
FC = FF // 128
E = 8
CAP = 5121.0
EPS = 1e-5
RD = 1.0 / D

_ONES = 0
_L128 = 128
_IDENT = 256
_IOTA8 = 384
_BIG8 = 392
_NEG8 = 400
_ONESROW = 408
_EPSC = 920
_DIF8 = 928
_ZROW = 936
_EROW = 1448
NCF = 2472


def host_consts():
    import ml_dtypes
    cf = np.zeros((128, NCF), np.float32)
    cf[:, _ONES:_ONES + 128] = 1.0
    cf[:, _L128:_L128 + 128] = np.fromfunction(
        lambda p, m: (p < m).astype(np.float32), (128, 128))
    cf[:, _IDENT:_IDENT + 128] = np.eye(128, dtype=np.float32)
    cf[:, _IOTA8:_IOTA8 + 8] = np.arange(8, dtype=np.float32)[None, :]
    cf[:, _BIG8:_BIG8 + 8] = 99.0
    cf[:, _NEG8:_NEG8 + 8] = -1.0
    cf[:, _ONESROW:_ONESROW + 512] = 1.0
    cf[:, _EPSC] = EPS
    cf[:, _DIF8:_DIF8 + 8] = 99.0 - np.arange(8, dtype=np.float32)[None, :]
    for e_ in range(8):
        cf[e_, _EROW + e_ * 128:_EROW + (e_ + 1) * 128] = 1.0
    em = np.fromfunction(lambda k, c: ((c - 384.0) >= k), (128, 896))
    em = em.astype(ml_dtypes.bfloat16)
    onesb = np.ones((128, 128), ml_dtypes.bfloat16)
    return cf, em, onesb


class Seq:

    def __init__(self, nc, es):
        self.sem = {}
        self.cnt = {}
        for name in ("pe", "dve", "act", "ld", "st"):
            self.sem[name] = es.enter_context(nc.semaphore("s_" + name))
            self.cnt[name] = 0

    def mark(self, name, instr, n=1):
        instr.then_inc(self.sem[name], n)
        self.cnt[name] += n
        return instr

    def snap(self):
        return dict(self.cnt)

    def wait(self, eng, snap, only=None):
        for name, v in snap.items():
            if only is not None and name not in only:
                continue
            if v > 0:
                eng.wait_ge(self.sem[name], v)

    def wait_at(self, eng, name, val):
        if val > 0:
            eng.wait_ge(self.sem[name], val)



def build_la(lgt_bypass=False, skip=()):
    nc = bass.Bass("TRN2", target_bir_lowering=False)
    xh = nc.dram_tensor("x", [D, T], F32, kind="ExternalInput").ap()
    wqh = nc.dram_tensor("wq", [D, D], F32, kind="ExternalInput").ap()
    wkh = nc.dram_tensor("wk", [D, D], F32, kind="ExternalInput").ap()
    wvh = nc.dram_tensor("wv", [D, D], F32, kind="ExternalInput").ap()
    woh = nc.dram_tensor("wo", [D, D], F32, kind="ExternalInput").ap()
    boh = nc.dram_tensor("bo", [128, CH], F32, kind="ExternalInput").ap()
    rwh = nc.dram_tensor("rw", [128, CH * E], F32, kind="ExternalInput").ap()
    cfh = nc.dram_tensor("cf", [128, NCF], F32, kind="ExternalInput").ap()
    emh = nc.dram_tensor("em", [128, 896], BF16, kind="ExternalInput").ap()
    onbh = nc.dram_tensor("onb", [128, 128], BF16, kind="ExternalInput").ap()

    x1h = nc.dram_tensor("x1", [D, T], F32, kind="ExternalOutput").ap()
    oh1h = nc.dram_tensor("oh1o", [128, NTT * E], F32, kind="ExternalOutput").ap()
    oh2h = nc.dram_tensor("oh2o", [128, NTT * E], F32, kind="ExternalOutput").ap()
    g12h = nc.dram_tensor("g12o", [128, 2 * NTT], F32, kind="ExternalOutput").ap()
    inclh = nc.dram_tensor("inclo", [1, 128], F32, kind="ExternalOutput").ap()
    dbg1h = nc.dram_tensor("dbg1", [128, NTT * E], F32, kind="ExternalOutput").ap()
    dbg2h = nc.dram_tensor("dbg2", [8, T], F32, kind="ExternalOutput").ap()
    dbg3h = nc.dram_tensor("dbg3", [128, NTT * 6], F32, kind="ExternalOutput").ap()
    lgtinh = nc.dram_tensor("lgtin", [128, NTT * E], F32, kind="ExternalInput").ap() if lgt_bypass else None
    dbg4h = nc.dram_tensor("dbg4", [128, NCF], F32, kind="ExternalOutput").ap()

    es = ExitStack()
    mk = es.enter_context
    xln = mk(nc.sbuf_tensor("xln", [128, CH * T], BF16))
    vpad = mk(nc.sbuf_tensor("vpad", [128, NTT * H * 65], BF16))
    oT = mk(nc.sbuf_tensor("oT", [128, CH * T], BF16))
    qk = mk(nc.sbuf_tensor("qk", [128, 2 * T], BF16))
    wa = mk(nc.sbuf_tensor("wa", [128, 2 * CH * D], BF16))
    xsl = mk(nc.sbuf_tensor("xsl", [128, CH * 512], F32))
    lnr = mk(nc.sbuf_tensor("lnr", [1, 4 * 512], F32))
    sq = mk(nc.sbuf_tensor("sq", [128, 512], F32))
    wt = mk(nc.sbuf_tensor("wt", [128, 2 * 512], BF16))
    rden = mk(nc.sbuf_tensor("rden", [1, 512], F32))
    bcs = mk(nc.sbuf_tensor("bcs", [64, 512], F32))
    nrm = mk(nc.sbuf_tensor("nrm", [1, 16], F32))
    iar = mk(nc.sbuf_tensor("iar", [1, 2 * 512], F32))
    cf = mk(nc.sbuf_tensor("cf_s", [128, NCF], F32))
    em = mk(nc.sbuf_tensor("em_s", [128, 896], BF16))
    onb = mk(nc.sbuf_tensor("onb_s", [128, 128], BF16))
    bo = mk(nc.sbuf_tensor("bo_s", [128, CH], F32))
    rw = mk(nc.sbuf_tensor("rw_s", [128, CH * E], F32))
    lg = mk(nc.sbuf_tensor("lg", [8, 128], F32))
    lgt = mk(nc.sbuf_tensor("lgt", [128, NTT * E], F32))
    oh1 = mk(nc.sbuf_tensor("oh1", [128, NTT * E], F32))
    oh2 = mk(nc.sbuf_tensor("oh2", [128, NTT * E], F32))
    oh12 = mk(nc.sbuf_tensor("oh12", [128, NTT * E], F32))
    g12 = mk(nc.sbuf_tensor("g12", [128, 2 * NTT], F32))
    gsc = mk(nc.sbuf_tensor("gsc", [128, 5 * E], F32))
    gcl = mk(nc.sbuf_tensor("gcl", [128, 8], F32))
    trow = mk(nc.sbuf_tensor("trow", [1, 2 * 128], F32))
    dbg2 = mk(nc.sbuf_tensor("dbg2s", [8, T], F32))
    dbg3 = mk(nc.sbuf_tensor("dbg3s", [128, NTT * 6], F32))

    p0 = mk(nc.psum_tensor("p0", [128, 512], F32))
    p1 = mk(nc.psum_tensor("p1", [128, 512], F32))
    po = mk(nc.psum_tensor("po", [65, 512], F32))
    pt = mk(nc.psum_tensor("pt", [128, 512], F32))
    pl = mk(nc.psum_tensor("pl", [8, T], F32))

    seq = Seq(nc, es)
    blk = mk(nc.Block())
    SLD, SST = seq.sem["ld"], seq.sem["st"]
    SPE, SDVE, SACT = seq.sem["pe"], seq.sem["dve"], seq.sem["act"]

    ONESC = cf[:, _ONES:_ONES + 1]
    ONESR = cf[0:1, _ONES:_ONES + 128]
    ONESROW512 = cf[0:1, _ONESROW:_ONESROW + 512]
    IOTA8 = cf[:, _IOTA8:_IOTA8 + 8]
    BIG8 = cf[:, _BIG8:_BIG8 + 8]
    NEG8 = cf[:, _NEG8:_NEG8 + 8]
    vv = vpad[:].rearrange("p (t h d) -> p t h d", t=NTT, h=H)

    def ld(g, dst, src):
        g.dma_start(out=dst, in_=src).then_inc(SLD, 16)
        seq.cnt["ld"] += 16

    def st(s, dst, src):
        s.dma_start(out=dst, in_=src).then_inc(SST, 16)
        seq.cnt["st"] += 16

    @blk.gpsimd
    def _(g):
        ld(g, cf[:], cfh)
        ld(g, em[:], emh)
        ld(g, onb[:], onbh)
        ld(g, bo[:], boh)
        ld(g, rw[:], rwh)
        if "ldw" not in skip:
            ld(g, wa[:, 0:CH * D].rearrange("p (c o) -> p c o", c=CH), wvh.rearrange("(c p) o -> p c o", p=128))
    lds_wv = seq.cnt["ld"]

    for sl in range(NTR) if 'p1' not in skip else []:
        s0 = seq.snap()

        @blk.gpsimd
        def _(g):
            seq.wait(g, s0, only=("dve", "act", "pe"))
            for c in range(CH):
                ld(g, xsl[:, c * 512:(c + 1) * 512],
                   xh[c * 128:(c + 1) * 128, sl * 512:(sl + 1) * 512])
        lds = seq.cnt["ld"]

        pe0 = seq.cnt["pe"]
        a0 = seq.cnt["act"]

        @blk.scalar
        def _(s):
            s.wait_ge(SLD, lds)
            seq.wait(s, s0, only=("pe", "dve"))
            for c in range(CH):
                if c >= 1:
                    s.wait_ge(SPE, pe0 + c)
                seq.mark("act", s.square(out=sq[:], in_=xsl[:, c * 512:(c + 1) * 512]))

        @blk.tensor
        def _(t):
            t.wait_ge(SLD, lds)
            seq.wait(t, s0, only=("act", "dve"))
            for c in range(CH):
                t.matmul(po[0:1, :], ONESC, xsl[:, c * 512:(c + 1) * 512],
                         start=(c == 0), stop=(c == CH - 1))
                t.wait_ge(SACT, a0 + c + 1)
                seq.mark("pe", t.matmul(pt[0:1, :], ONESC, sq[:],
                                        start=(c == 0), stop=(c == CH - 1)))
        pe_st = seq.cnt["pe"]

        sM = slice(0, 512)
        sV = slice(512, 1024)
        sR = slice(1024, 1536)
        sMR = slice(1536, 2048)

        @blk.scalar
        def _(s):
            s.wait_ge(SPE, pe_st)
            s.mul(out=lnr[0:1, sM], in_=po[0:1, :], mul=RD)
            seq.mark("act", s.mul(out=lnr[0:1, sV], in_=pt[0:1, :], mul=RD))
        a1 = seq.cnt["act"]

        @blk.vector
        def _(v):
            v.wait_ge(SACT, a1)
            seq.mark("dve", v.tensor_mul(lnr[0:1, sR], lnr[0:1, sM], lnr[0:1, sM]))
            v.wait_ge(SDVE, seq.cnt["dve"])
            seq.mark("dve", v.tensor_sub(lnr[0:1, sV], lnr[0:1, sV], lnr[0:1, sR]))
        d1 = seq.cnt["dve"]

        @blk.scalar
        def _(s):
            s.wait_ge(SDVE, d1)
            seq.mark("act", s.activation(out=lnr[0:1, sR], in_=lnr[0:1, sV], func=AF.Ln,
                                         bias=cf[0:1, _EPSC:_EPSC + 1]))
            s.wait_ge(SACT, seq.cnt["act"])
            seq.mark("act", s.activation(out=lnr[0:1, sR], in_=lnr[0:1, sR],
                                         func=AF.Exp, scale=-0.5))
        a2 = seq.cnt["act"]

        @blk.vector
        def _(v):
            v.wait_ge(SACT, a2)
            seq.mark("dve", v.tensor_mul(lnr[0:1, sMR], lnr[0:1, sM], lnr[0:1, sR]))
        d2 = seq.cnt["dve"]

        @blk.tensor
        def _(t):
            t.wait_ge(SDVE, d2)
            t.matmul(p0[:], ONESR, lnr[0:1, sR], start=True, stop=True)
            seq.mark("pe", t.matmul(p1[:], ONESR, lnr[0:1, sMR], start=True, stop=True))
        pe_bc = seq.cnt["pe"]

        @blk.vector
        def _(v):
            v.wait_ge(SPE, pe_bc)
            for c in range(CH):
                seq.mark("dve", v.tensor_mul(sq[:], xsl[:, c * 512:(c + 1) * 512], p0[:]))
                v.wait_ge(SDVE, seq.cnt["dve"])
                seq.mark("dve", v.tensor_sub(
                    xln[:, c * T + sl * 512:c * T + (sl + 1) * 512], sq[:], p1[:]))
                v.wait_ge(SDVE, seq.cnt["dve"])

    ln0_done = seq.snap()

    @blk.vector
    def _(v):
        seq.wait(v, ln0_done, only=("pe",))
        if "p2m" not in skip:
            v.memset(vpad[:], 0.0)
            seq.mark("dve", v.memset(vv[:, :, :, 64:65], 1.0))
        else:
            seq.mark("dve", v.engine_nop())
    d_ones = seq.cnt["dve"]

    evac = []
    for i, (tt, orr) in enumerate([(a, b) for a in range(NTT) for b in range(2)] if 'p2' not in skip else []):
        ps = p0 if i % 2 == 0 else p1

        @blk.tensor
        def _(t):
            if i == 0:
                t.wait_ge(SLD, lds_wv)
                seq.wait(t, ln0_done, only=("dve",))
                t.wait_ge(SDVE, d_ones)
            if i >= 2:
                nm, cc = evac[i - 2]
                t.wait_ge(seq.sem[nm], cc)
            for c in range(CH):
                m = t.matmul(ps[:], xln[:, c * T + tt * 128:c * T + (tt + 1) * 128],
                             wa[:, c * D + orr * 512:c * D + (orr + 1) * 512],
                             start=(c == 0), stop=(c == CH - 1))
            seq.mark("pe", m)
        pev = seq.cnt["pe"]

        if i % 2 == 0:
            @blk.vector
            def _(v):
                v.wait_ge(SPE, pev)
                seq.mark("dve", v.tensor_copy(
                    out=vv[:, tt, orr * 8:(orr + 1) * 8, 0:64], in_=ps[:]))
            evac.append(("dve", seq.cnt["dve"]))
        else:
            @blk.scalar
            def _(s):
                s.wait_ge(SPE, pev)
                seq.mark("act", s.copy(
                    out=vv[:, tt, orr * 8:(orr + 1) * 8, 0:64], in_=ps[:]))
            evac.append(("act", seq.cnt["act"]))
    v_done = seq.snap()

    @blk.gpsimd
    def _(g):
        seq.wait(g, v_done, only=("pe",))
        if "ldw" not in skip:
            ld(g, wa[:, 0:CH * D].rearrange("p (c o) -> p c o", c=CH), wqh.rearrange("(c p) o -> p c o", p=128))
            ld(g, wa[:, CH * D:2 * CH * D].rearrange("p (c o) -> p c o", c=CH), wkh.rearrange("(c p) o -> p c o", p=128))
    lds_qk = seq.cnt["ld"]

    for c in range(CH) if 'p3' not in skip else []:
        prev_scores = seq.snap()
        evq = []
        for i in range(2 * NTR):
            which, tr = divmod(i, NTR)
            base = which * CH * D
            ps = p0 if i % 2 == 0 else p1

            @blk.tensor
            def _(t):
                if i == 0:
                    t.wait_ge(SLD, lds_qk)
                    seq.wait(t, prev_scores, only=("dve", "act"))
                if i >= 2:
                    nm, cc = evq[i - 2]
                    t.wait_ge(seq.sem[nm], cc)
                for ic in range(CH):
                    m = t.matmul(ps[:],
                                 wa[:, base + ic * D + c * 128:base + ic * D + (c + 1) * 128],
                                 xln[:, ic * T + tr * 512:ic * T + (tr + 1) * 512],
                                 start=(ic == 0), stop=(ic == CH - 1))
                seq.mark("pe", m)
            pej = seq.cnt["pe"]

            dst = qk[:, which * T + tr * 512:which * T + (tr + 1) * 512]
            if i % 2 == 0:
                @blk.vector
                def _(v):
                    v.wait_ge(SPE, pej)
                    seq.mark("dve", v.tensor_copy(out=dst, in_=ps[:]))
                evq.append(("dve", seq.cnt["dve"]))
            else:
                @blk.scalar
                def _(s):
                    s.wait_ge(SPE, pej)
                    seq.mark("act", s.copy(out=dst, in_=ps[:]))
                evq.append(("act", seq.cnt["act"]))
        qk_done = seq.snap()

        d_prev_red = seq.cnt["dve"]
        for j in range(2):
            for tr in range(NTR):
                s1 = seq.snap()

                @blk.scalar
                def _(s):
                    if j == 0 and tr == 0:
                        seq.wait(s, qk_done, only=("dve", "act", "pe"))
                    else:
                        seq.wait(s, s1, only=("pe",))
                    seq.mark("act", s.square(
                        out=sq[:], in_=qk[:, j * T + tr * 512:j * T + (tr + 1) * 512]))
                aN = seq.cnt["act"]

                @blk.tensor
                def _(t):
                    t.wait_ge(SACT, aN)
                    t.wait_ge(SDVE, d_prev_red)
                    for hh in range(2):
                        ps_n = pt if hh == 0 else po
                        m = t.matmul(ps_n[0:1, 0:512],
                                     cf[hh * 64:hh * 64 + 64, _ONES:_ONES + 1],
                                     sq[hh * 64:hh * 64 + 64, :], start=True, stop=True)
                        seq.mark("pe", m)
                peN2 = seq.cnt["pe"]

                @blk.vector
                def _(v):
                    v.wait_ge(SPE, peN2)
                    for hh in range(2):
                        ps_n = pt if hh == 0 else po
                        sl8 = 8 + hh
                        seq.mark("dve", v.reduce_max(out=nrm[0:1, sl8:sl8 + 1],
                                                     in_=ps_n[0:1, 0:512], axis=AX.X))
                        v.wait_ge(SDVE, seq.cnt["dve"])
                        tgt = 2 * j + hh
                        if tr == 0:
                            m = v.tensor_copy(out=nrm[0:1, tgt:tgt + 1], in_=nrm[0:1, sl8:sl8 + 1])
                        else:
                            m = v.tensor_max(nrm[0:1, tgt:tgt + 1], nrm[0:1, tgt:tgt + 1],
                                             nrm[0:1, sl8:sl8 + 1])
                        seq.mark("dve", m)
                        v.wait_ge(SDVE, seq.cnt["dve"])
                d_prev_red = seq.cnt["dve"]

        @blk.vector
        def _(v):
            seq.mark("dve", v.tensor_mul(nrm[0:1, 4:6], nrm[0:1, 0:2], nrm[0:1, 2:4]))
        dN = seq.cnt["dve"]

        @blk.scalar
        def _(s):
            s.wait_ge(SDVE, dN)
            seq.mark("act", s.activation(out=nrm[0:1, 6:8], in_=nrm[0:1, 4:6], func=AF.Ln))
            s.wait_ge(SACT, seq.cnt["act"])
            seq.mark("act", s.activation(out=nrm[0:1, 6:8], in_=nrm[0:1, 6:8],
                                         func=AF.Exp, scale=0.5))
        aI = seq.cnt["act"]

        @blk.vector
        def _(v):
            v.wait_ge(SACT, aI)
            v.wait_ge(SPE, seq.cnt['pe'])
            for hh in range(2):
                m = v.tensor_scalar_mul(iar[0:1, hh * 512:(hh + 1) * 512],
                                        ONESROW512, nrm[0:1, 6 + hh:7 + hh])
            seq.mark("dve", m)
        d_iar = seq.cnt["dve"]

        if c == 0:
            prev_grp_epis = []
            prev_out_mark = 0
        for hh in range(2):
            qb = hh * 64
            habs = 2 * c + hh
            for qr in range(NTR):
                nkc = 4 * qr + 4
                epi = {}
                for kc in range(nkc):
                    ps = p0 if kc % 2 == 0 else p1
                    wslot = wt[:, (kc % 2) * 512:((kc % 2) + 1) * 512]

                    @blk.tensor
                    def _(t):
                        if kc == 0:
                            t.wait_ge(SDVE, d_iar)
                        if kc < 2:
                            for nm, cc in prev_grp_epis:
                                t.wait_ge(seq.sem[nm], cc)
                        if kc >= 2:
                            nm, cc = epi[kc - 2]
                            t.wait_ge(seq.sem[nm], cc)
                        t.matmul(ps[:], qk[qb:qb + 64, T + kc * 128:T + (kc + 1) * 128],
                                 qk[qb:qb + 64, qr * 512:(qr + 1) * 512],
                                 start=True, stop=False)
                        m = t.matmul(ps[:], ONESR, iar[0:1, hh * 512:(hh + 1) * 512],
                                     start=False, stop=True)
                        seq.mark("pe", m)
                        if kc >= 1:
                            kp = kc - 1
                            nm, cc = epi[kp]
                            t.wait_ge(seq.sem[nm], cc)
                            if kp == 0:
                                t.wait_ge(SDVE, prev_out_mark)
                            m2 = t.matmul(po[:], vv[:, kp, habs, :],
                                          wt[:, (kp % 2) * 512:((kp % 2) + 1) * 512],
                                          start=(kp == 0), stop=False)
                            seq.mark("pe", m2)
                    pe_s = seq.cnt["pe"]

                    diag = kc >= 4 * qr
                    if diag:
                        dd = kc * 128 - qr * 512
                        msl = em[:, 384 - dd:384 - dd + 512]

                        @blk.vector
                        def _(v):
                            v.wait_ge(SPE, pe_s)
                            seq.mark("dve", v.tensor_mul(wslot, ps[:], msl))
                        epi[kc] = ("dve", seq.cnt["dve"])
                    elif kc % 2 == 0:
                        @blk.vector
                        def _(v):
                            v.wait_ge(SPE, pe_s)
                            seq.mark("dve", v.tensor_copy(out=wslot, in_=ps[:]))
                        epi[kc] = ("dve", seq.cnt["dve"])
                    else:
                        @blk.scalar
                        def _(s):
                            s.wait_ge(SPE, pe_s)
                            seq.mark("act", s.copy(out=wslot, in_=ps[:]))
                        epi[kc] = ("act", seq.cnt["act"])

                kp = nkc - 1

                @blk.tensor
                def _(t):
                    nm, cc = epi[kp]
                    t.wait_ge(seq.sem[nm], cc)
                    seq.mark("pe", t.matmul(po[:], vv[:, kp, habs, :],
                                            wt[:, (kp % 2) * 512:((kp % 2) + 1) * 512],
                                            start=(kp == 0), stop=True))
                pe_av = seq.cnt["pe"]

                @blk.vector
                def _(v):
                    v.wait_ge(SPE, pe_av)
                    seq.mark("dve", v.reciprocal(out=rden[0:1, :], in_=po[64:65, :]))
                d_r = seq.cnt["dve"]

                @blk.tensor
                def _(t):
                    t.wait_ge(SDVE, d_r)
                    seq.mark("pe", t.matmul(pt[0:64, :], ONESR[0:1, 0:64], rden[0:1, :],
                                            start=True, stop=True))
                pe_b = seq.cnt["pe"]

                @blk.scalar
                def _(s):
                    s.wait_ge(SPE, pe_b)
                    seq.mark("act", s.copy(out=bcs[:], in_=pt[0:64, :]))
                a_b = seq.cnt["act"]

                @blk.vector
                def _(v):
                    v.wait_ge(SACT, a_b)
                    seq.mark("dve", v.tensor_mul(
                        oT[qb:qb + 64, c * T + qr * 512:c * T + (qr + 1) * 512],
                        po[0:64, :], bcs[:]))
                prev_out_mark = seq.cnt["dve"]
                prev_grp_epis = [epi[nkc - 1]] + ([epi[nkc - 2]] if nkc >= 2 else [])

    attn_done = seq.snap()

    @blk.gpsimd
    def _(g):
        seq.wait(g, attn_done, only=("pe",))
        if "ldw" not in skip:
            ld(g, wa[:, 0:CH * D].rearrange("p (c o) -> p c o", c=CH), woh.rearrange("(c p) o -> p c o", p=128))
    lds_wo = seq.cnt["ld"]

    prev_apply_mark = 0
    for tr in range(NTR) if 'p4' not in skip else []:
        xld = []
        s0 = seq.snap()

        @blk.gpsimd
        def _(g):
            seq.wait(g, s0, only=("act", "pe", "dve", "st"))
            for oc in range(CH):
                ld(g, xsl[:, oc * 512:(oc + 1) * 512],
                   xh[oc * 128:(oc + 1) * 128, tr * 512:(tr + 1) * 512])
                xld.append(seq.cnt["ld"])
        xld_all = xld[-1]

        d_x1_marks = []
        for oc in range(CH):
            ps = p0 if oc % 2 == 0 else p1

            @blk.tensor
            def _(t):
                if tr == 0 and oc == 0:
                    t.wait_ge(SLD, lds_wo)
                    seq.wait(t, attn_done, only=("dve", "act"))
                if oc < 2:
                    seq.wait(t, s0, only=("dve", "act", "st"))
                else:
                    t.wait_ge(SDVE, d_x1_marks[oc - 2])
                for c2 in range(CH):
                    t.matmul(ps[:],
                             wa[:, c2 * D + oc * 128:c2 * D + (oc + 1) * 128],
                             oT[:, c2 * T + tr * 512:c2 * T + (tr + 1) * 512],
                             start=(c2 == 0), stop=False)
                seq.mark("pe", t.matmul(ps[:], ONESR, cf[0:1, _ZROW:_ZROW + 512],
                                        start=False, stop=True))
            pe_y = seq.cnt["pe"]

            @blk.vector
            def _(v):
                v.wait_ge(SPE, pe_y)
                v.wait_ge(SLD, xld_all)
                seq.mark("dve", v.scalar_tensor_tensor(
                    out=xsl[:, oc * 512:(oc + 1) * 512], in0=ps[:],
                    scalar=bo[:, oc:oc + 1], in1=xsl[:, oc * 512:(oc + 1) * 512],
                    op0=ALU.add, op1=ALU.add))
            d_x1 = seq.cnt["dve"]
            d_x1_marks.append(d_x1)

            @blk.sync
            def _(s):
                s.wait_ge(SDVE, d_x1)
                st(s, x1h[oc * 128:(oc + 1) * 128, tr * 512:(tr + 1) * 512],
                   xsl[:, oc * 512:(oc + 1) * 512])

            s2 = seq.snap()

            @blk.scalar
            def _(s):
                s.wait_ge(SDVE, d_x1)
                seq.wait(s, s2, only=("pe",))
                seq.mark("act", s.square(out=sq[:], in_=xsl[:, oc * 512:(oc + 1) * 512]))
            a_sq = seq.cnt["act"]

            @blk.tensor
            def _(t):
                t.wait_ge(SDVE, d_x1)
                t.matmul(po[0:1, :], ONESC, xsl[:, oc * 512:(oc + 1) * 512],
                         start=(oc == 0), stop=(oc == CH - 1))
                t.wait_ge(SACT, a_sq)
                seq.mark("pe", t.matmul(pt[0:1, :], ONESC, sq[:],
                                        start=(oc == 0), stop=(oc == CH - 1)))
        pe_st1 = seq.cnt["pe"]

        sM = slice(0, 512)
        sV = slice(512, 1024)
        sR = slice(1024, 1536)
        sMR = slice(1536, 2048)

        @blk.scalar
        def _(s):
            s.wait_ge(SPE, pe_st1)
            s.mul(out=lnr[0:1, sM], in_=po[0:1, :], mul=RD)
            seq.mark("act", s.mul(out=lnr[0:1, sV], in_=pt[0:1, :], mul=RD))
        a1 = seq.cnt["act"]

        @blk.vector
        def _(v):
            v.wait_ge(SACT, a1)
            seq.mark("dve", v.tensor_mul(lnr[0:1, sR], lnr[0:1, sM], lnr[0:1, sM]))
            v.wait_ge(SDVE, seq.cnt["dve"])
            seq.mark("dve", v.tensor_sub(lnr[0:1, sV], lnr[0:1, sV], lnr[0:1, sR]))
        d1 = seq.cnt["dve"]

        @blk.scalar
        def _(s):
            s.wait_ge(SDVE, d1)
            seq.mark("act", s.activation(out=lnr[0:1, sR], in_=lnr[0:1, sV], func=AF.Ln,
                                         bias=cf[0:1, _EPSC:_EPSC + 1]))
            s.wait_ge(SACT, seq.cnt["act"])
            seq.mark("act", s.activation(out=lnr[0:1, sR], in_=lnr[0:1, sR],
                                         func=AF.Exp, scale=-0.5))
        a2 = seq.cnt["act"]

        @blk.vector
        def _(v):
            v.wait_ge(SACT, a2)
            seq.mark("dve", v.tensor_mul(lnr[0:1, sMR], lnr[0:1, sM], lnr[0:1, sR]))
        d2 = seq.cnt["dve"]

        @blk.tensor
        def _(t):
            t.wait_ge(SDVE, d2)
            t.matmul(p0[:], ONESR, lnr[0:1, sR], start=True, stop=True)
            seq.mark("pe", t.matmul(p1[:], ONESR, lnr[0:1, sMR], start=True, stop=True))
        pe_bc1 = seq.cnt["pe"]

        for oc in range(CH):
            @blk.vector
            def _(v):
                if oc == 0:
                    v.wait_ge(SPE, pe_bc1)
                seq.mark("dve", v.tensor_mul(sq[:], xsl[:, oc * 512:(oc + 1) * 512], p0[:]))
                v.wait_ge(SDVE, seq.cnt["dve"])
                seq.mark("dve", v.tensor_sub(sq[:], sq[:], p1[:]))
            d_ln = seq.cnt["dve"]

            @blk.tensor
            def _(t):
                t.wait_ge(SDVE, d_ln)
                seq.mark("pe", t.matmul(pl[:, tr * 512:(tr + 1) * 512],
                                        rw[:, oc * E:(oc + 1) * E], sq[:],
                                        start=(oc == 0), stop=(oc == CH - 1)))
            pe_rt = seq.cnt["pe"]

            @blk.vector
            def _(v):
                v.wait_ge(SPE, pe_rt)
                v.engine_nop()
            prev_apply_mark = seq.cnt["dve"]

    router_done = seq.snap()

    @blk.vector
    def _(v):
        v.wait_ge(SPE, router_done["pe"])
        seq.mark("dve", v.tensor_copy(out=dbg2[:], in_=pl[:]))

    tp_evac = {}
    d_lgc = {}
    router_pe = router_done["pe"]
    if lgt_bypass:
        @blk.gpsimd
        def _(g):
            ld(g, lgt[:], lgtinh)
        lgt_ld = seq.cnt["ld"]

        @blk.vector
        def _(v):
            v.wait_ge(SLD, lgt_ld)
            seq.mark("dve", v.engine_nop())
        tp_evac[NTT - 1] = seq.cnt["dve"]
    for tt in range(NTT) if not lgt_bypass else []:
        @blk.vector
        def _(v):
            if tt == 0:
                v.wait_ge(SPE, router_pe)
            seq.mark("dve", v.tensor_copy(out=lg[:], in_=pl[:, tt * 128:(tt + 1) * 128]))
        d_lgc[tt] = seq.cnt["dve"]

        @blk.tensor
        def _(t):
            seq.wait_at(t, "dve", d_lgc[tt])
            if tt > 0:
                seq.wait_at(t, "dve", tp_evac[tt - 1])
            seq.mark("pe", t.transpose(pt[:, 0:8], lg[:],
                                       cf[0:8, _IDENT:_IDENT + 8]))
        pe_tp = seq.cnt["pe"]

        @blk.vector
        def _(v):
            v.wait_ge(SPE, pe_tp)
            seq.mark("dve", v.tensor_copy(out=lgt[:, tt * E:(tt + 1) * E], in_=pt[:, 0:8]))
        tp_evac[tt] = seq.cnt["dve"]

    for tt in range(NTT):
        te = slice(tt * E, (tt + 1) * E)

        @blk.vector
        def _(v):
            if tt == 0:
                seq.wait_at(v, "dve", tp_evac[NTT - 1])
            seq.mark("dve", v.reduce_max(out=gcl[:, 0:1], in_=lgt[:, te], axis=AX.X))
            v.wait_ge(SDVE, seq.cnt["dve"])
            seq.mark("dve", v.tensor_scalar_mul(gcl[:, 1:2], gcl[:, 0:1], -1.0))
        d_m = seq.cnt["dve"]

        @blk.scalar
        def _(s):
            s.wait_ge(SDVE, d_m)
            seq.mark("act", s.activation(out=gsc[:, 0:E], in_=lgt[:, te],
                                         func=AF.Exp, bias=gcl[:, 1:2]))
        a_e = seq.cnt["act"]

        @blk.vector
        def _(v):
            def mw(instr):
                seq.mark("dve", instr)
                v.wait_ge(SDVE, seq.cnt["dve"])
            v.wait_ge(SACT, a_e)
            if "dbg" not in skip:
                mw(v.tensor_copy(out=dbg3[:, tt * 6:tt * 6 + 1], in_=gcl[:, 0:1]))

            mw(v.reduce_sum(out=gcl[:, 2:3], in_=gsc[:, 0:E], axis=AX.X))
            if "dbg" not in skip:
                mw(v.tensor_copy(out=dbg3[:, tt * 6 + 1:tt * 6 + 2], in_=gcl[:, 2:3]))

            mw(v.reciprocal(out=gcl[:, 3:4], in_=gcl[:, 2:3]))
            mw(v.tensor_scalar_mul(gsc[:, E:2 * E], gsc[:, 0:E], gcl[:, 3:4]))
            DIF8 = cf[:, _DIF8:_DIF8 + 8]
            mw(v.reduce_max(out=gcl[:, 0:1], in_=gsc[:, E:2 * E], axis=AX.X))
            mw(v.tensor_scalar(out=gsc[:, 2 * E:3 * E], in0=gsc[:, E:2 * E],
                               scalar1=gcl[:, 0:1], scalar2=None, op0=ALU.is_equal))
            mw(v.tensor_mul(gsc[:, 3 * E:4 * E], gsc[:, 2 * E:3 * E], DIF8))
            mw(v.tensor_sub(gsc[:, 3 * E:4 * E], BIG8, gsc[:, 3 * E:4 * E]))
            mw(v.tensor_reduce(out=gcl[:, 4:5], in_=gsc[:, 3 * E:4 * E],
                               axis=AX.X, op=ALU.min))
            if "dbg" not in skip:
                mw(v.tensor_copy(out=dbg3[:, tt * 6 + 2:tt * 6 + 3], in_=gcl[:, 0:1]))
                mw(v.tensor_copy(out=dbg3[:, tt * 6 + 3:tt * 6 + 4], in_=gcl[:, 4:5]))

            mw(v.tensor_scalar(out=oh1[:, te], in0=IOTA8, scalar1=gcl[:, 4:5],
                               scalar2=None, op0=ALU.is_equal))
            mw(v.tensor_scalar_add(gsc[:, 4 * E:5 * E], gsc[:, E:2 * E], 1.0))
            mw(v.tensor_mul(gsc[:, 4 * E:5 * E], gsc[:, 4 * E:5 * E], oh1[:, te]))
            mw(v.tensor_sub(gsc[:, 4 * E:5 * E], gsc[:, E:2 * E], gsc[:, 4 * E:5 * E]))
            mw(v.reduce_max(out=gcl[:, 5:6], in_=gsc[:, 4 * E:5 * E], axis=AX.X))
            mw(v.tensor_scalar(out=gsc[:, 2 * E:3 * E], in0=gsc[:, 4 * E:5 * E],
                               scalar1=gcl[:, 5:6], scalar2=None, op0=ALU.is_equal))
            mw(v.tensor_mul(gsc[:, 3 * E:4 * E], gsc[:, 2 * E:3 * E], DIF8))
            mw(v.tensor_sub(gsc[:, 3 * E:4 * E], BIG8, gsc[:, 3 * E:4 * E]))
            mw(v.tensor_reduce(out=gcl[:, 6:7], in_=gsc[:, 3 * E:4 * E],
                               axis=AX.X, op=ALU.min))
            if "dbg" not in skip:
                mw(v.tensor_copy(out=dbg3[:, tt * 6 + 4:tt * 6 + 5], in_=gcl[:, 5:6]))
                mw(v.tensor_copy(out=dbg3[:, tt * 6 + 5:tt * 6 + 6], in_=gcl[:, 6:7]))

            mw(v.tensor_scalar(out=oh2[:, te], in0=IOTA8, scalar1=gcl[:, 6:7],
                               scalar2=None, op0=ALU.is_equal))
            mw(v.tensor_add(gcl[:, 7:8], gcl[:, 0:1], gcl[:, 5:6]))
            mw(v.reciprocal(out=gcl[:, 7:8], in_=gcl[:, 7:8]))
            mw(v.tensor_mul(g12[:, tt:tt + 1], gcl[:, 0:1], gcl[:, 7:8]))
            mw(v.tensor_mul(g12[:, NTT + tt:NTT + tt + 1], gcl[:, 5:6], gcl[:, 7:8]))
            seq.mark("dve", v.tensor_add(oh12[:, te], oh1[:, te], oh2[:, te]))
    d_gate = seq.cnt["dve"]

    @blk.tensor
    def _(t):
        t.wait_ge(SDVE, d_gate)
        seq.mark("pe", t.matmul(pt[0:1, 0:128], ONESC, oh12[:], start=True, stop=True))
    pe_tt = seq.cnt["pe"]

    @blk.vector
    def _(v):
        v.wait_ge(SPE, pe_tt)
        v.tensor_copy(out=trow[0:1, 0:128], in_=pt[0:1, 0:128])
        a, b = 0, 128
        last = None
        for sft in (8, 16, 32, 64):
            seq.mark("dve", v.tensor_copy(out=trow[0:1, b:b + sft], in_=trow[0:1, a:a + sft]))
            v.wait_ge(SDVE, seq.cnt["dve"])
            seq.mark("dve", v.tensor_add(trow[0:1, b + sft:b + 128],
                                         trow[0:1, a + sft:a + 128],
                                         trow[0:1, a:a + 128 - sft]))
            v.wait_ge(SDVE, seq.cnt["dve"])
            last = v.engine_nop()
            a, b = b, a
        if a != 0:
            last = v.tensor_copy(out=trow[0:1, 0:128], in_=trow[0:1, a:a + 128])
        seq.mark("dve", last)
    d_scan = seq.cnt["dve"]

    @blk.sync
    def _(s):
        s.wait_ge(SDVE, d_scan)
        st(s, oh1h, oh1[:])
        st(s, oh2h, oh2[:])
        st(s, g12h, g12[:])
        st(s, inclh, trow[0:1, 0:128])
        st(s, dbg1h, lgt[:])
        st(s, dbg2h, dbg2[:])
        st(s, dbg3h, dbg3[:])
        st(s, dbg4h, cf[:])
        s.wait_ge(SST, seq.cnt["st"])

    es.close()
    return nc



def _prep_la_inputs(inp):
    cfc, emc, onbc = host_consts()
    wqT = np.ascontiguousarray(inp["L0_attn_Wq"].T)
    wkT = np.ascontiguousarray(inp["L0_attn_Wk"].T)
    wvT = np.ascontiguousarray(inp["L0_attn_Wv"].T)
    woT = np.ascontiguousarray(inp["L0_attn_Wo"].T)
    boc = np.ascontiguousarray(inp["L0_attn_bo"].reshape(CH, 128).T)
    rwc = np.ascontiguousarray(
        inp["L0_ffn_router_w"].T.reshape(CH, 128, E).transpose(1, 0, 2).reshape(128, CH * E))
    X = inp["L0_attn_X"]
    maps = []
    for b in range(8):
        maps.append({
            "x": np.ascontiguousarray(X[b].T),
            "wq": wqT, "wk": wkT, "wv": wvT, "wo": woT,
            "bo": boc, "rw": rwc, "cf": cfc, "em": emc, "onb": onbc,
        })
    return maps





def build_lb():
    nc = bass.Bass("TRN2", target_bir_lowering=False)
    x1h = nc.dram_tensor("x1", [D, T], F32, kind="ExternalInput").ap()
    oh1h = nc.dram_tensor("oh1i", [128, NTT * E], F32, kind="ExternalInput").ap()
    oh2h = nc.dram_tensor("oh2i", [128, NTT * E], F32, kind="ExternalInput").ap()
    g12h = nc.dram_tensor("g12i", [128, 2 * NTT], F32, kind="ExternalInput").ap()
    inclh = nc.dram_tensor("incli", [1, 128], F32, kind="ExternalInput").ap()
    cnth = nc.dram_tensor("cnts", [8, 8], F32, kind="ExternalInput").ap()
    cmh = nc.dram_tensor("cmask", [8, 1], F32, kind="ExternalInput").ap()
    w1h = nc.dram_tensor("w1s", [E, FC, 128, CH * 128], F32, kind="ExternalInput").ap()
    w2h = nc.dram_tensor("w2r", [E, CH, 128, FC * 128], F32, kind="ExternalInput").ap()
    cfh = nc.dram_tensor("cf", [128, NCF], F32, kind="ExternalInput").ap()
    x2h = nc.dram_tensor("x2", [D, T], F32, kind="ExternalOutput").ap()

    es = ExitStack()
    mk = es.enter_context
    xs = mk(nc.sbuf_tensor("xs", [128, CH * 512], F32))
    comb = mk(nc.sbuf_tensor("comb", [128, CH * 1024], F32))
    xln = mk(nc.sbuf_tensor("xln", [128, CH * T], BF16))
    hbuf = mk(nc.sbuf_tensor("hbuf", [128, FC * 1024], BF16))
    w1b = mk(nc.sbuf_tensor("w1b", [128, 2 * 1024], BF16))
    w2b = mk(nc.sbuf_tensor("w2b", [128, 2 * FC * 128], BF16))
    gbc = mk(nc.sbuf_tensor("gbc", [128, 1024], F32))
    ysc = mk(nc.sbuf_tensor("ysc", [128, 512], F32))
    sq = mk(nc.sbuf_tensor("sq", [128, 512], F32))
    t3 = mk(nc.sbuf_tensor("t3", [128, 512], F32))
    lnr = mk(nc.sbuf_tensor("lnr", [1, 4 * 512], F32))
    cf = mk(nc.sbuf_tensor("cf_s", [128, NCF], F32))
    oh1 = mk(nc.sbuf_tensor("oh1", [128, NTT * E], F32))
    oh2 = mk(nc.sbuf_tensor("oh2", [128, NTT * E], F32))
    oh12 = mk(nc.sbuf_tensor("oh12", [128, NTT * E], F32))
    g12 = mk(nc.sbuf_tensor("g12", [128, 2 * NTT], F32))
    incl = mk(nc.sbuf_tensor("incl", [1, 128], F32))
    cnts = mk(nc.sbuf_tensor("cnts_s", [8, 8], F32))
    cmask = mk(nc.sbuf_tensor("cmask_s", [8, 1], F32))
    cbrow = mk(nc.sbuf_tensor("cbrow", [1, 8], F32))
    gtok = mk(nc.sbuf_tensor("gtok", [128, NTT * E], F32))
    gfeat = mk(nc.sbuf_tensor("gfeat", [8, T], F32))
    gsc = mk(nc.sbuf_tensor("gsc", [128, 3 * E], F32))
    gcl = mk(nc.sbuf_tensor("gcl", [128, 8], F32))

    p0 = mk(nc.psum_tensor("p0", [128, 512], F32))
    p1 = mk(nc.psum_tensor("p1", [128, 512], F32))
    po = mk(nc.psum_tensor("po", [128, 512], F32))
    pt = mk(nc.psum_tensor("pt", [128, 512], F32))

    seq = Seq(nc, es)
    blk = mk(nc.Block())
    SLD, SST = seq.sem["ld"], seq.sem["st"]
    SPE, SDVE, SACT = seq.sem["pe"], seq.sem["dve"], seq.sem["act"]

    ONESC = cf[:, _ONES:_ONES + 1]
    ONESR = cf[0:1, _ONES:_ONES + 128]
    L128 = cf[:, _L128:_L128 + 128]

    def ld(g, dst, src):
        g.dma_start(out=dst, in_=src).then_inc(SLD, 16)
        seq.cnt["ld"] += 16

    def st(s, dst, src):
        s.dma_start(out=dst, in_=src).then_inc(SST, 16)
        seq.cnt["st"] += 16

    @blk.gpsimd
    def _(g):
        ld(g, cf[:], cfh)
        ld(g, oh1[:], oh1h)
        ld(g, oh2[:], oh2h)
        ld(g, g12[:], g12h)
        ld(g, incl[:], inclh)
        ld(g, cnts[:], cnth)
        ld(g, cmask[:], cmh)
    lds0 = seq.cnt["ld"]

    for sl in range(NTR):
        s0 = seq.snap()

        @blk.gpsimd
        def _(g):
            seq.wait(g, s0, only=("act", "pe", "dve", "st"))
            for c in range(CH):
                ld(g, xs[:, c * 512:(c + 1) * 512],
                   x1h[c * 128:(c + 1) * 128, sl * 512:(sl + 1) * 512])
        lds = seq.cnt["ld"]

        pe0 = seq.cnt["pe"]
        a0 = seq.cnt["act"]

        @blk.scalar
        def _(s):
            s.wait_ge(SLD, lds)
            seq.wait(s, s0, only=("pe", "dve"))
            for c in range(CH):
                if c >= 1:
                    s.wait_ge(SPE, pe0 + c)
                seq.mark("act", s.square(out=sq[:], in_=xs[:, c * 512:(c + 1) * 512]))

        @blk.tensor
        def _(t):
            t.wait_ge(SLD, lds)
            seq.wait(t, s0, only=("act", "dve"))
            for c in range(CH):
                t.matmul(po[0:1, :], ONESC, xs[:, c * 512:(c + 1) * 512],
                         start=(c == 0), stop=(c == CH - 1))
                t.wait_ge(SACT, a0 + c + 1)
                seq.mark("pe", t.matmul(pt[0:1, :], ONESC, sq[:],
                                        start=(c == 0), stop=(c == CH - 1)))
        pe_st = seq.cnt["pe"]

        sM, sV, sR, sMR = slice(0, 512), slice(512, 1024), slice(1024, 1536), slice(1536, 2048)

        @blk.scalar
        def _(s):
            s.wait_ge(SPE, pe_st)
            seq.mark("act", s.mul(out=lnr[0:1, sM], in_=po[0:1, :], mul=RD))
            s.wait_ge(SACT, seq.cnt["act"])
            seq.mark("act", s.mul(out=lnr[0:1, sV], in_=pt[0:1, :], mul=RD))
        a1 = seq.cnt["act"]

        @blk.vector
        def _(v):
            v.wait_ge(SACT, a1)
            seq.mark("dve", v.tensor_mul(lnr[0:1, sR], lnr[0:1, sM], lnr[0:1, sM]))
            v.wait_ge(SDVE, seq.cnt["dve"])
            seq.mark("dve", v.tensor_sub(lnr[0:1, sV], lnr[0:1, sV], lnr[0:1, sR]))
        d1 = seq.cnt["dve"]

        @blk.scalar
        def _(s):
            s.wait_ge(SDVE, d1)
            seq.mark("act", s.activation(out=lnr[0:1, sR], in_=lnr[0:1, sV], func=AF.Ln,
                                         bias=cf[0:1, _EPSC:_EPSC + 1]))
            s.wait_ge(SACT, seq.cnt["act"])
            seq.mark("act", s.activation(out=lnr[0:1, sR], in_=lnr[0:1, sR],
                                         func=AF.Exp, scale=-0.5))
        a2 = seq.cnt["act"]

        @blk.vector
        def _(v):
            v.wait_ge(SACT, a2)
            seq.mark("dve", v.tensor_mul(lnr[0:1, sMR], lnr[0:1, sM], lnr[0:1, sR]))
        d2 = seq.cnt["dve"]

        @blk.tensor
        def _(t):
            t.wait_ge(SDVE, d2)
            t.matmul(p0[:], ONESR, lnr[0:1, sR], start=True, stop=True)
            seq.mark("pe", t.matmul(p1[:], ONESR, lnr[0:1, sMR], start=True, stop=True))
        pe_bc = seq.cnt["pe"]

        for c in range(CH):
            @blk.vector
            def _(v):
                if c == 0:
                    v.wait_ge(SPE, pe_bc)
                xsl_ = xs[:, c * 512:(c + 1) * 512]
                seq.mark("dve", v.tensor_mul(sq[:], xsl_, p0[:]))
                v.wait_ge(SDVE, seq.cnt["dve"])
                seq.mark("dve", v.tensor_sub(xln[:, c * T + sl * 512:c * T + (sl + 1) * 512],
                                             sq[:], p1[:]))
                seq.mark("dve", v.tensor_sub(t3[:], sq[:], p1[:]))
                v.wait_ge(SDVE, seq.cnt["dve"])
                seq.mark("dve", v.tensor_add(xsl_, xsl_, t3[:]))
                v.wait_ge(SDVE, seq.cnt["dve"])
            d_ap = seq.cnt["dve"]

            @blk.sync
            def _(s):
                s.wait_ge(SDVE, d_ap)
                st(s, x2h[c * 128:(c + 1) * 128, sl * 512:(sl + 1) * 512],
                   xs[:, c * 512:(c + 1) * 512])
    ln_done = seq.snap()

    @blk.tensor
    def _(t):
        t.wait_ge(SLD, lds0)
        seq.wait(t, ln_done, only=("dve",))
        seq.mark("pe", t.matmul(pt[0:1, 0:8], cmask[:, 0:1], cnts[:], start=True, stop=True))
    pe_cb = seq.cnt["pe"]

    @blk.vector
    def _(v):
        v.wait_ge(SPE, pe_cb)
        seq.mark("dve", v.tensor_copy(out=cbrow[:], in_=pt[0:1, 0:8]))
        v.wait_ge(SDVE, seq.cnt["dve"])
        seq.mark("dve", v.tensor_add(oh12[:], oh1[:], oh2[:]))
    d_cb = seq.cnt["dve"]

    d_gt = d_cb
    for tt in range(NTT):
        te = slice(tt * E, (tt + 1) * E)

        @blk.tensor
        def _(t):
            t.wait_ge(SDVE, d_gt)
            t.matmul(pt[:, 0:8], L128, oh12[:, te], start=True, stop=False)
            if tt > 0:
                t.matmul(pt[:, 0:8], ONESR, cbrow[:], start=False, stop=False)
                seq.mark("pe", t.matmul(pt[:, 0:8], ONESR,
                                        incl[0:1, (tt - 1) * 8:tt * 8],
                                        start=False, stop=True))
            else:
                seq.mark("pe", t.matmul(pt[:, 0:8], ONESR, cbrow[:],
                                        start=False, stop=True))
        pe_P = seq.cnt["pe"]

        @blk.vector
        def _(v):
            def mw(instr):
                seq.mark("dve", instr)
                v.wait_ge(SDVE, seq.cnt["dve"])
            v.wait_ge(SPE, pe_P)
            mw(v.tensor_mul(gsc[:, 0:E], pt[:, 0:8], oh1[:, te]))
            mw(v.reduce_sum(out=gcl[:, 0:1], in_=gsc[:, 0:E], axis=AX.X))
            mw(v.tensor_single_scalar(out=gcl[:, 1:2], in_=gcl[:, 0:1], scalar=CAP, op=ALU.is_lt))
            mw(v.tensor_mul(gcl[:, 2:3], gcl[:, 1:2], g12[:, tt:tt + 1]))
            mw(v.tensor_mul(gsc[:, 0:E], pt[:, 0:8], oh2[:, te]))
            mw(v.reduce_sum(out=gcl[:, 3:4], in_=gsc[:, 0:E], axis=AX.X))
            mw(v.tensor_single_scalar(out=gcl[:, 4:5], in_=gcl[:, 3:4], scalar=CAP, op=ALU.is_lt))
            mw(v.tensor_mul(gcl[:, 5:6], gcl[:, 4:5], g12[:, NTT + tt:NTT + tt + 1]))
            mw(v.tensor_scalar_mul(gsc[:, E:2 * E], oh1[:, te], gcl[:, 2:3]))
            mw(v.tensor_scalar_mul(gsc[:, 2 * E:3 * E], oh2[:, te], gcl[:, 5:6]))
            mw(v.tensor_add(gtok[:, te], gsc[:, E:2 * E], gsc[:, 2 * E:3 * E]))
        d_gt = seq.cnt["dve"]

    tpv = {}
    for tt in range(NTT):
        @blk.tensor
        def _(t):
            if tt == 0:
                t.wait_ge(SDVE, d_gt)
            else:
                t.wait_ge(SDVE, tpv[tt - 1])
            seq.mark("pe", t.transpose(pt[0:8, 0:128], gtok[:, tt * E:(tt + 1) * E],
                                       cf[0:128, _IDENT:_IDENT + 128]))
        pe_tp = seq.cnt["pe"]

        @blk.vector
        def _(v):
            v.wait_ge(SPE, pe_tp)
            seq.mark("dve", v.tensor_copy(out=gfeat[:, tt * 128:(tt + 1) * 128],
                                          in_=pt[0:8, 0:128]))
        tpv[tt] = seq.cnt["dve"]
    gate_done = seq.snap()

    psum_hist = {}
    pidx = [0]

    def next_psum():
        i = pidx[0]
        pidx[0] += 1
        return (p0 if i % 2 == 0 else p1), i

    def psum_guard(t, i):
        if i >= 2:
            nm, cc = psum_hist[i - 2]
            t.wait_ge(seq.sem[nm], cc)

    d_gbc_copy = 0
    slot_cons = {("w1", 0): 0, ("w1", 1): 0, ("w2", 0): 0, ("w2", 1): 0}
    first_h = [True]

    for tr2 in range(2):
        s_z = seq.snap()

        @blk.vector
        def _(v):
            seq.wait(v, s_z, only=("pe",))
            seq.mark("dve", v.memset(comb[:], 0.0))
        d_comb0 = seq.cnt["dve"]

        for e in range(E):
            tb = tr2 * 1024
            for sub in range(2):
                @blk.tensor
                def _(t):
                    if tr2 == 0 and e == 0 and sub == 0:
                        seq.wait(t, gate_done, only=("dve",))
                    t.wait_ge(SDVE, d_gbc_copy)
                    seq.mark("pe", t.matmul(
                        po[:], cf[0:8, _EROW + e * 128:_EROW + (e + 1) * 128],
                        gfeat[0:8, tb + sub * 512:tb + (sub + 1) * 512],
                        start=True, stop=True))
                pe_g = seq.cnt["pe"]

                @blk.vector
                def _(v):
                    v.wait_ge(SPE, pe_g)
                    seq.mark("dve", v.tensor_copy(
                        out=gbc[:, sub * 512:(sub + 1) * 512], in_=po[:]))
                d_gbc_copy = seq.cnt["dve"]

            for oc in range(FC):
                slot = oc % 2

                @blk.gpsimd
                def _(g):
                    seq.wait_at(g, "pe", slot_cons[("w1", slot)])
                    ld(g, w1b[:, slot * 1024:(slot + 1) * 1024], w1h[e, oc])
                w1ld = seq.cnt["ld"]

                for sub in range(2):
                    ps, pi = next_psum()

                    @blk.tensor
                    def _(t):
                        t.wait_ge(SLD, w1ld)
                        if first_h[0]:
                            seq.wait(t, ln_done, only=("dve",))
                            first_h[0] = False
                        psum_guard(t, pi)
                        for c in range(CH):
                            m = t.matmul(ps[:],
                                         w1b[:, slot * 1024 + c * 128:slot * 1024 + (c + 1) * 128],
                                         xln[:, c * T + tb + sub * 512:c * T + tb + (sub + 1) * 512],
                                         start=(c == 0), stop=(c == CH - 1))
                        seq.mark("pe", m)
                    pe_h = seq.cnt["pe"]
                    slot_cons[("w1", slot)] = pe_h

                    @blk.scalar
                    def _(s):
                        s.wait_ge(SPE, pe_h)
                        seq.mark("act", s.activation(
                            out=hbuf[:, oc * 1024 + sub * 512:oc * 1024 + (sub + 1) * 512],
                            in_=ps[:], func=AF.Silu))
                    psum_hist[pi] = ("act", seq.cnt["act"])
            a_h = seq.cnt["act"]

            for oc8 in range(CH):
                slot = oc8 % 2

                @blk.gpsimd
                def _(g):
                    seq.wait_at(g, "pe", slot_cons[("w2", slot)])
                    ld(g, w2b[:, slot * FC * 128:(slot + 1) * FC * 128], w2h[e, oc8])
                w2ld = seq.cnt["ld"]

                for sub in range(2):
                    ps, pi = next_psum()

                    @blk.tensor
                    def _(t):
                        t.wait_ge(SLD, w2ld)
                        t.wait_ge(SACT, a_h)
                        psum_guard(t, pi)
                        for fc in range(FC):
                            m = t.matmul(ps[:],
                                         w2b[:, slot * FC * 128 + fc * 128:slot * FC * 128 + (fc + 1) * 128],
                                         hbuf[:, fc * 1024 + sub * 512:fc * 1024 + (sub + 1) * 512],
                                         start=(fc == 0), stop=(fc == FC - 1))
                        seq.mark("pe", m)
                    pe_y = seq.cnt["pe"]
                    slot_cons[("w2", slot)] = pe_y

                    @blk.vector
                    def _(v):
                        v.wait_ge(SPE, pe_y)
                        if e == 0 and oc8 == 0 and sub == 0:
                            v.wait_ge(SDVE, d_comb0)
                        seq.mark("dve", v.tensor_mul(ysc[:], ps[:], gbc[:, sub * 512:(sub + 1) * 512]))
                        v.wait_ge(SDVE, seq.cnt["dve"])
                        csl = comb[:, oc8 * 1024 + sub * 512:oc8 * 1024 + (sub + 1) * 512]
                        seq.mark("dve", v.tensor_add(csl, csl, ysc[:]))
                        v.wait_ge(SDVE, seq.cnt["dve"])
                    psum_hist[pi] = ("dve", seq.cnt["dve"])

        for grp in range(2):
            s_f = seq.snap()

            @blk.gpsimd
            def _(g):
                seq.wait(g, s_f, only=("st", "dve", "act", "pe"))
                for i8 in range(8):
                    oc8, sub = (grp * 8 + i8) // 2, (grp * 8 + i8) % 2
                    ld(g, xs[:, i8 * 512:(i8 + 1) * 512],
                       x2h[oc8 * 128:(oc8 + 1) * 128,
                           tr2 * 1024 + sub * 512:tr2 * 1024 + (sub + 1) * 512])
            ld_fin = seq.cnt["ld"]

            for i8 in range(8):
                oc8, sub = (grp * 8 + i8) // 2, (grp * 8 + i8) % 2

                @blk.vector
                def _(v):
                    v.wait_ge(SLD, ld_fin)
                    xsl_ = xs[:, i8 * 512:(i8 + 1) * 512]
                    seq.mark("dve", v.tensor_add(
                        xsl_, xsl_,
                        comb[:, oc8 * 1024 + sub * 512:oc8 * 1024 + (sub + 1) * 512]))
                    v.wait_ge(SDVE, seq.cnt["dve"])
                d_fx = seq.cnt["dve"]

                @blk.sync
                def _(s):
                    s.wait_ge(SDVE, d_fx)
                    st(s, x2h[oc8 * 128:(oc8 + 1) * 128,
                              tr2 * 1024 + sub * 512:tr2 * 1024 + (sub + 1) * 512],
                       xs[:, i8 * 512:(i8 + 1) * 512])

    @blk.sync
    def _(s):
        s.wait_ge(SST, seq.cnt["st"])

    es.close()
    return nc


def _prep_lb_shared(inp):
    cfc, _, _ = host_consts()
    w1 = inp["L0_ffn_w1"]
    w2 = inp["L0_ffn_w2"]
    w1s = np.ascontiguousarray(
        w1.reshape(E, CH, 128, FC, 128).transpose(0, 3, 2, 1, 4).reshape(E, FC, 128, CH * 128))
    w2r = np.ascontiguousarray(
        w2.reshape(E, FC, 128, CH, 128).transpose(0, 3, 2, 1, 4).reshape(E, CH, 128, FC * 128))
    return cfc, w1s, w2r


def _prep_lb_inputs(inp, la_results):
    cfc, w1s, w2r = _prep_lb_shared(inp)
    cnts = np.stack([la_results[b]["inclo"][0, 120:128] for b in range(8)])
    maps = []
    for b in range(8):
        cm = np.zeros((8, 1), np.float32)
        cm[:b] = 1.0
        maps.append({
            "x1": la_results[b]["x1"],
            "oh1i": la_results[b]["oh1o"],
            "oh2i": la_results[b]["oh2o"],
            "g12i": la_results[b]["g12o"],
            "incli": la_results[b]["inclo"],
            "cnts": cnts, "cmask": cm,
            "w1s": w1s, "w2r": w2r, "cf": cfc,
        })
    return maps



def build_lc():
    nc = bass.Bass("TRN2", target_bir_lowering=False)
    xh = nc.dram_tensor("x2", [D, T], F32, kind="ExternalInput").ap()
    wqh = nc.dram_tensor("wq", [D, D], F32, kind="ExternalInput").ap()
    wkh = nc.dram_tensor("wk", [D, D], F32, kind="ExternalInput").ap()
    wvh = nc.dram_tensor("wv", [D, D], F32, kind="ExternalInput").ap()
    woh = nc.dram_tensor("wo", [D, D], F32, kind="ExternalInput").ap()
    bqkh = nc.dram_tensor("bqk", [128, 2 * CH], F32, kind="ExternalInput").ap()
    bvh = nc.dram_tensor("bv", [1, D], F32, kind="ExternalInput").ap()
    boh = nc.dram_tensor("bo", [128, CH], F32, kind="ExternalInput").ap()
    cfh = nc.dram_tensor("cf", [128, NCF], F32, kind="ExternalInput").ap()
    onbh = nc.dram_tensor("onb", [128, 128], BF16, kind="ExternalInput").ap()
    x3h = nc.dram_tensor("x3", [D, T], F32, kind="ExternalOutput").ap()

    es = ExitStack()
    mk = es.enter_context
    xln = mk(nc.sbuf_tensor("xln", [128, CH * T], BF16))
    vpad = mk(nc.sbuf_tensor("vpad", [128, NTT * H * 65], BF16))
    oT = mk(nc.sbuf_tensor("oT", [128, CH * T], BF16))
    qk = mk(nc.sbuf_tensor("qk", [128, 2 * T], BF16))
    wa = mk(nc.sbuf_tensor("wa", [128, 2 * CH * D], BF16))
    xsl = mk(nc.sbuf_tensor("xsl", [128, CH * 512], F32))
    lnr = mk(nc.sbuf_tensor("lnr", [1, 4 * 512], F32))
    sq = mk(nc.sbuf_tensor("sq", [128, 512], F32))
    wt = mk(nc.sbuf_tensor("wt", [128, 2 * 512], BF16))
    rden = mk(nc.sbuf_tensor("rden", [1, 512], F32))
    bcs = mk(nc.sbuf_tensor("bcs", [64, 512], F32))
    nrm = mk(nc.sbuf_tensor("nrm", [1, 16], F32))
    acol = mk(nc.sbuf_tensor("acol", [128, 2], F32))
    cf = mk(nc.sbuf_tensor("cf_s", [128, NCF], F32))
    onb = mk(nc.sbuf_tensor("onb_s", [128, 128], BF16))
    bqk = mk(nc.sbuf_tensor("bqk_s", [128, 2 * CH], F32))
    bvr = mk(nc.sbuf_tensor("bv_s", [1, D], BF16))
    bo = mk(nc.sbuf_tensor("bo_s", [128, CH], F32))

    p0 = mk(nc.psum_tensor("p0", [128, 512], F32))
    p1 = mk(nc.psum_tensor("p1", [128, 512], F32))
    po = mk(nc.psum_tensor("po", [65, 512], F32))
    pt = mk(nc.psum_tensor("pt", [128, 512], F32))

    seq = Seq(nc, es)
    blk = mk(nc.Block())
    SLD, SST = seq.sem["ld"], seq.sem["st"]
    SPE, SDVE, SACT = seq.sem["pe"], seq.sem["dve"], seq.sem["act"]

    ONESC = cf[:, _ONES:_ONES + 1]
    ONESR = cf[0:1, _ONES:_ONES + 128]
    ONBR = onb[0:1, :]
    vv = vpad[:].rearrange("p (t h d) -> p t h d", t=NTT, h=H)

    def ld(g, dst, src):
        g.dma_start(out=dst, in_=src).then_inc(SLD, 16)
        seq.cnt["ld"] += 16

    def st(s, dst, src):
        s.dma_start(out=dst, in_=src).then_inc(SST, 16)
        seq.cnt["st"] += 16

    @blk.gpsimd
    def _(g):
        ld(g, cf[:], cfh)
        ld(g, onb[:], onbh)
        ld(g, bqk[:], bqkh)
        ld(g, bvr[:], bvh)
        ld(g, bo[:], boh)
        ld(g, wa[:, 0:CH * D].rearrange("p (c o) -> p c o", c=CH),
           wvh.rearrange("(c p) o -> p c o", p=128))
    lds_wv = seq.cnt["ld"]

    for sl in range(NTR):
        s0 = seq.snap()

        @blk.gpsimd
        def _(g):
            seq.wait(g, s0, only=("dve", "act", "pe"))
            for c in range(CH):
                ld(g, xsl[:, c * 512:(c + 1) * 512],
                   xh[c * 128:(c + 1) * 128, sl * 512:(sl + 1) * 512])
        lds = seq.cnt["ld"]

        pe0 = seq.cnt["pe"]
        a0 = seq.cnt["act"]

        @blk.scalar
        def _(s):
            s.wait_ge(SLD, lds)
            seq.wait(s, s0, only=("pe", "dve"))
            for c in range(CH):
                if c >= 1:
                    s.wait_ge(SPE, pe0 + c)
                seq.mark("act", s.square(out=sq[:], in_=xsl[:, c * 512:(c + 1) * 512]))

        @blk.tensor
        def _(t):
            t.wait_ge(SLD, lds)
            seq.wait(t, s0, only=("act", "dve"))
            for c in range(CH):
                t.matmul(po[0:1, :], ONESC, xsl[:, c * 512:(c + 1) * 512],
                         start=(c == 0), stop=(c == CH - 1))
                t.wait_ge(SACT, a0 + c + 1)
                seq.mark("pe", t.matmul(pt[0:1, :], ONESC, sq[:],
                                        start=(c == 0), stop=(c == CH - 1)))
        pe_st = seq.cnt["pe"]

        sM, sV, sR, sMR = slice(0, 512), slice(512, 1024), slice(1024, 1536), slice(1536, 2048)

        @blk.scalar
        def _(s):
            s.wait_ge(SPE, pe_st)
            seq.mark("act", s.mul(out=lnr[0:1, sM], in_=po[0:1, :], mul=RD))
            s.wait_ge(SACT, seq.cnt["act"])
            seq.mark("act", s.mul(out=lnr[0:1, sV], in_=pt[0:1, :], mul=RD))
        a1 = seq.cnt["act"]

        @blk.vector
        def _(v):
            v.wait_ge(SACT, a1)
            seq.mark("dve", v.tensor_mul(lnr[0:1, sR], lnr[0:1, sM], lnr[0:1, sM]))
            v.wait_ge(SDVE, seq.cnt["dve"])
            seq.mark("dve", v.tensor_sub(lnr[0:1, sV], lnr[0:1, sV], lnr[0:1, sR]))
        d1 = seq.cnt["dve"]

        @blk.scalar
        def _(s):
            s.wait_ge(SDVE, d1)
            seq.mark("act", s.activation(out=lnr[0:1, sR], in_=lnr[0:1, sV], func=AF.Ln,
                                         bias=cf[0:1, _EPSC:_EPSC + 1]))
            s.wait_ge(SACT, seq.cnt["act"])
            seq.mark("act", s.activation(out=lnr[0:1, sR], in_=lnr[0:1, sR],
                                         func=AF.Exp, scale=-0.5))
        a2 = seq.cnt["act"]

        @blk.vector
        def _(v):
            v.wait_ge(SACT, a2)
            seq.mark("dve", v.tensor_mul(lnr[0:1, sMR], lnr[0:1, sM], lnr[0:1, sR]))
        d2 = seq.cnt["dve"]

        @blk.tensor
        def _(t):
            t.wait_ge(SDVE, d2)
            t.matmul(p0[:], ONESR, lnr[0:1, sR], start=True, stop=True)
            seq.mark("pe", t.matmul(p1[:], ONESR, lnr[0:1, sMR], start=True, stop=True))
        pe_bc = seq.cnt["pe"]

        @blk.vector
        def _(v):
            v.wait_ge(SPE, pe_bc)
            for c in range(CH):
                seq.mark("dve", v.tensor_mul(sq[:], xsl[:, c * 512:(c + 1) * 512], p0[:]))
                v.wait_ge(SDVE, seq.cnt["dve"])
                seq.mark("dve", v.tensor_sub(xln[:, c * T + sl * 512:c * T + (sl + 1) * 512],
                                             sq[:], p1[:]))
                v.wait_ge(SDVE, seq.cnt["dve"])
    ln_done = seq.snap()

    @blk.vector
    def _(v):
        seq.wait(v, ln_done, only=("pe",))
        v.memset(vpad[:], 0.0)
        seq.mark("dve", v.memset(vv[:, :, :, 64:65], 1.0))
    d_ones = seq.cnt["dve"]

    evac = []
    for i, (tt, orr) in enumerate([(a, b) for a in range(NTT) for b in range(2)]):
        ps = p0 if i % 2 == 0 else p1

        @blk.tensor
        def _(t):
            if i == 0:
                t.wait_ge(SLD, lds_wv)
                seq.wait(t, ln_done, only=("dve",))
                t.wait_ge(SDVE, d_ones)
            if i >= 2:
                nm, cc = evac[i - 2]
                t.wait_ge(seq.sem[nm], cc)
            for c in range(CH):
                t.matmul(ps[:], xln[:, c * T + tt * 128:c * T + (tt + 1) * 128],
                         wa[:, c * D + orr * 512:c * D + (orr + 1) * 512],
                         start=(c == 0), stop=False)
            seq.mark("pe", t.matmul(ps[:], ONBR, bvr[0:1, orr * 512:(orr + 1) * 512],
                                    start=False, stop=True))
        pev = seq.cnt["pe"]

        if i % 2 == 0:
            @blk.vector
            def _(v):
                v.wait_ge(SPE, pev)
                seq.mark("dve", v.tensor_copy(
                    out=vv[:, tt, orr * 8:(orr + 1) * 8, 0:64], in_=ps[:]))
            evac.append(("dve", seq.cnt["dve"]))
        else:
            @blk.scalar
            def _(s):
                s.wait_ge(SPE, pev)
                seq.mark("act", s.copy(
                    out=vv[:, tt, orr * 8:(orr + 1) * 8, 0:64], in_=ps[:]))
            evac.append(("act", seq.cnt["act"]))
    v_done = seq.snap()

    @blk.gpsimd
    def _(g):
        seq.wait(g, v_done, only=("pe",))
        ld(g, wa[:, 0:CH * D].rearrange("p (c o) -> p c o", c=CH),
           wqh.rearrange("(c p) o -> p c o", p=128))
        ld(g, wa[:, CH * D:2 * CH * D].rearrange("p (c o) -> p c o", c=CH),
           wkh.rearrange("(c p) o -> p c o", p=128))
    lds_qk = seq.cnt["ld"]

    prev_grp_epis = []
    prev_out_mark = 0
    for c in range(CH):
        prev_scores = seq.snap()
        evq = []
        for i in range(2 * NTR):
            which, tr = divmod(i, NTR)
            base = which * CH * D
            ps = p0 if i % 2 == 0 else p1

            @blk.tensor
            def _(t):
                if i == 0:
                    t.wait_ge(SLD, lds_qk)
                    seq.wait(t, prev_scores, only=("dve", "act"))
                if i >= 2:
                    nm, cc = evq[i - 2]
                    t.wait_ge(seq.sem[nm], cc)
                for ic in range(CH):
                    m = t.matmul(ps[:],
                                 wa[:, base + ic * D + c * 128:base + ic * D + (c + 1) * 128],
                                 xln[:, ic * T + tr * 512:ic * T + (tr + 1) * 512],
                                 start=(ic == 0), stop=(ic == CH - 1))
                seq.mark("pe", m)
            pej = seq.cnt["pe"]

            dst = qk[:, which * T + tr * 512:which * T + (tr + 1) * 512]
            bias_ap = bqk[:, which * CH + c:which * CH + c + 1]

            @blk.vector
            def _(v):
                v.wait_ge(SPE, pej)
                seq.mark("dve", v.tensor_scalar(out=dst, in0=ps[:], scalar1=bias_ap,
                                                scalar2=None, op0=ALU.add))
            evq.append(("dve", seq.cnt["dve"]))
        qk_done = seq.snap()

        d_prev_red = seq.cnt["dve"]
        for j in range(2):
            for tr in range(NTR):
                s1 = seq.snap()

                @blk.scalar
                def _(s):
                    if j == 0 and tr == 0:
                        seq.wait(s, qk_done, only=("dve", "act", "pe"))
                    else:
                        seq.wait(s, s1, only=("pe",))
                    seq.mark("act", s.square(
                        out=sq[:], in_=qk[:, j * T + tr * 512:j * T + (tr + 1) * 512]))
                aN = seq.cnt["act"]

                @blk.tensor
                def _(t):
                    t.wait_ge(SACT, aN)
                    t.wait_ge(SDVE, d_prev_red)
                    for hh in range(2):
                        ps_n = pt if hh == 0 else po
                        seq.mark("pe", t.matmul(ps_n[0:1, 0:512],
                                                cf[hh * 64:hh * 64 + 64, _ONES:_ONES + 1],
                                                sq[hh * 64:hh * 64 + 64, :],
                                                start=True, stop=True))
                peN2 = seq.cnt["pe"]

                @blk.vector
                def _(v):
                    v.wait_ge(SPE, peN2)
                    for hh in range(2):
                        ps_n = pt if hh == 0 else po
                        sl8 = 8 + hh
                        seq.mark("dve", v.reduce_max(out=nrm[0:1, sl8:sl8 + 1],
                                                     in_=ps_n[0:1, 0:512], axis=AX.X))
                        v.wait_ge(SDVE, seq.cnt["dve"])
                        tgt = 2 * j + hh
                        if tr == 0:
                            m = v.tensor_copy(out=nrm[0:1, tgt:tgt + 1], in_=nrm[0:1, sl8:sl8 + 1])
                        else:
                            m = v.tensor_max(nrm[0:1, tgt:tgt + 1], nrm[0:1, tgt:tgt + 1],
                                             nrm[0:1, sl8:sl8 + 1])
                        seq.mark("dve", m)
                        v.wait_ge(SDVE, seq.cnt["dve"])
                d_prev_red = seq.cnt["dve"]

        @blk.vector
        def _(v):
            seq.mark("dve", v.tensor_mul(nrm[0:1, 4:6], nrm[0:1, 0:2], nrm[0:1, 2:4]))
        dN = seq.cnt["dve"]

        @blk.scalar
        def _(s):
            s.wait_ge(SDVE, dN)
            seq.mark("act", s.activation(out=nrm[0:1, 6:8], in_=nrm[0:1, 4:6], func=AF.Ln))
            s.wait_ge(SACT, seq.cnt["act"])
            seq.mark("act", s.activation(out=nrm[0:1, 6:8], in_=nrm[0:1, 6:8],
                                         func=AF.Exp, scale=0.5))
        aI = seq.cnt["act"]

        @blk.vector
        def _(v):
            v.wait_ge(SACT, aI)
            v.wait_ge(SPE, seq.cnt["pe"])
            seq.mark("dve", v.tensor_scalar_mul(nrm[0:1, 8:10], nrm[0:1, 6:8], -0.125))
        d_nh = seq.cnt["dve"]

        @blk.tensor
        def _(t):
            t.wait_ge(SDVE, d_nh)
            t.matmul(pt[:, 0:1], ONESR, nrm[0:1, 8:9], start=True, stop=True)
            seq.mark("pe", t.matmul(pt[:, 1:2], ONESR, nrm[0:1, 9:10], start=True, stop=True))
        pe_nh = seq.cnt["pe"]

        @blk.vector
        def _(v):
            v.wait_ge(SPE, pe_nh)
            seq.mark("dve", v.tensor_copy(out=acol[:], in_=pt[:, 0:2]))
        d_acol = seq.cnt["dve"]

        for hh in range(2):
            qb = hh * 64
            habs = 2 * c + hh
            for qr in range(NTR):
                nkc = NTT
                epi = {}
                for kc in range(nkc):
                    ps = p0 if kc % 2 == 0 else p1
                    wslot = wt[:, (kc % 2) * 512:((kc % 2) + 1) * 512]

                    @blk.tensor
                    def _(t):
                        if kc == 0:
                            t.wait_ge(SDVE, d_acol)
                        if kc < 2:
                            for nm, cc in prev_grp_epis:
                                t.wait_ge(seq.sem[nm], cc)
                        if kc >= 2:
                            nm, cc = epi[kc - 2]
                            t.wait_ge(seq.sem[nm], cc)
                        seq.mark("pe", t.matmul(
                            ps[:], qk[qb:qb + 64, T + kc * 128:T + (kc + 1) * 128],
                            qk[qb:qb + 64, qr * 512:(qr + 1) * 512],
                            start=True, stop=True))
                        if kc >= 1:
                            kp = kc - 1
                            nm, cc = epi[kp]
                            t.wait_ge(seq.sem[nm], cc)
                            if kp == 0:
                                t.wait_ge(SDVE, prev_out_mark)
                            seq.mark("pe", t.matmul(po[:], vv[:, kp, habs, :],
                                                    wt[:, (kp % 2) * 512:((kp % 2) + 1) * 512],
                                                    start=(kp == 0), stop=False))
                    pe_s = seq.cnt["pe"]

                    @blk.scalar
                    def _(s):
                        s.wait_ge(SPE, pe_s)
                        seq.mark("act", s.activation(out=wslot, in_=ps[:], func=AF.Exp,
                                                     scale=0.125, bias=acol[:, hh:hh + 1]))
                    epi[kc] = ("act", seq.cnt["act"])

                kp = nkc - 1

                @blk.tensor
                def _(t):
                    nm, cc = epi[kp]
                    t.wait_ge(seq.sem[nm], cc)
                    seq.mark("pe", t.matmul(po[:], vv[:, kp, habs, :],
                                            wt[:, (kp % 2) * 512:((kp % 2) + 1) * 512],
                                            start=(kp == 0), stop=True))
                pe_av = seq.cnt["pe"]

                @blk.vector
                def _(v):
                    v.wait_ge(SPE, pe_av)
                    seq.mark("dve", v.reciprocal(out=rden[0:1, :], in_=po[64:65, :]))
                d_r = seq.cnt["dve"]

                @blk.tensor
                def _(t):
                    t.wait_ge(SDVE, d_r)
                    seq.mark("pe", t.matmul(pt[0:64, :], ONESR[0:1, 0:64], rden[0:1, :],
                                            start=True, stop=True))
                pe_b = seq.cnt["pe"]

                @blk.scalar
                def _(s):
                    s.wait_ge(SPE, pe_b)
                    seq.mark("act", s.copy(out=bcs[:], in_=pt[0:64, :]))
                a_b = seq.cnt["act"]

                @blk.vector
                def _(v):
                    v.wait_ge(SACT, a_b)
                    seq.mark("dve", v.tensor_mul(
                        oT[qb:qb + 64, c * T + qr * 512:c * T + (qr + 1) * 512],
                        po[0:64, :], bcs[:]))
                prev_out_mark = seq.cnt["dve"]
                prev_grp_epis = [epi[nkc - 1], epi[nkc - 2]]

    attn_done = seq.snap()

    @blk.gpsimd
    def _(g):
        seq.wait(g, attn_done, only=("pe",))
        ld(g, wa[:, 0:CH * D].rearrange("p (c o) -> p c o", c=CH),
           woh.rearrange("(c p) o -> p c o", p=128))
    lds_wo = seq.cnt["ld"]

    for tr in range(NTR):
        xld = []
        s0 = seq.snap()

        @blk.gpsimd
        def _(g):
            seq.wait(g, s0, only=("dve", "st"))
            for oc in range(CH):
                ld(g, xsl[:, oc * 512:(oc + 1) * 512],
                   xh[oc * 128:(oc + 1) * 128, tr * 512:(tr + 1) * 512])
                xld.append(seq.cnt["ld"])
        xld_all = xld[-1]

        d_x1_marks = []
        for oc in range(CH):
            ps = p0 if oc % 2 == 0 else p1

            @blk.tensor
            def _(t):
                if tr == 0 and oc == 0:
                    t.wait_ge(SLD, lds_wo)
                    seq.wait(t, attn_done, only=("dve", "act"))
                if oc >= 2:
                    t.wait_ge(SDVE, d_x1_marks[oc - 2])
                elif tr > 0:
                    seq.wait(t, s0, only=("dve",))
                for c2 in range(CH):
                    m = t.matmul(ps[:],
                                 wa[:, c2 * D + oc * 128:c2 * D + (oc + 1) * 128],
                                 oT[:, c2 * T + tr * 512:c2 * T + (tr + 1) * 512],
                                 start=(c2 == 0), stop=(c2 == CH - 1))
                seq.mark("pe", m)
            pe_y = seq.cnt["pe"]

            @blk.vector
            def _(v):
                v.wait_ge(SPE, pe_y)
                v.wait_ge(SLD, xld_all)
                seq.mark("dve", v.scalar_tensor_tensor(
                    out=xsl[:, oc * 512:(oc + 1) * 512], in0=ps[:],
                    scalar=bo[:, oc:oc + 1], in1=xsl[:, oc * 512:(oc + 1) * 512],
                    op0=ALU.add, op1=ALU.add))
                v.wait_ge(SDVE, seq.cnt["dve"])
            d_x1 = seq.cnt["dve"]
            d_x1_marks.append(d_x1)

            @blk.sync
            def _(s):
                s.wait_ge(SDVE, d_x1)
                st(s, x3h[oc * 128:(oc + 1) * 128, tr * 512:(tr + 1) * 512],
                   xsl[:, oc * 512:(oc + 1) * 512])

    @blk.sync
    def _(s):
        s.wait_ge(SST, seq.cnt["st"])

    es.close()
    return nc


def _prep_lc_inputs(inp, x2_list):
    cfc, emc, onbc = host_consts()
    wqT = np.ascontiguousarray(inp["L1_attn_Wq"].T)
    wkT = np.ascontiguousarray(inp["L1_attn_Wk"].T)
    wvT = np.ascontiguousarray(inp["L1_attn_Wv"].T)
    woT = np.ascontiguousarray(inp["L1_attn_Wo"].T)
    bqk = np.concatenate([inp["L1_attn_bq"].reshape(CH, 128).T,
                          inp["L1_attn_bk"].reshape(CH, 128).T], axis=1)
    bqk = np.ascontiguousarray(bqk)
    bv = np.ascontiguousarray(inp["L1_attn_bv"].reshape(1, D))
    boc = np.ascontiguousarray(inp["L1_attn_bo"].reshape(CH, 128).T)
    maps = []
    for b in range(8):
        maps.append({
            "x2": x2_list[b],
            "wq": wqT, "wk": wkT, "wv": wvT, "wo": woT,
            "bqk": bqk, "bv": bv, "bo": boc, "cf": cfc, "onb": onbc,
        })
    return maps



def build_ld():
    nc = bass.Bass("TRN2", target_bir_lowering=False)
    xh = nc.dram_tensor("x3", [D, T], F32, kind="ExternalInput").ap()
    k0ah = nc.dram_tensor("k0a", [FC, 128, CH * 128], U8, kind="ExternalInput").ap()
    k0bh = nc.dram_tensor("k0b", [FC, 128, CH * 128], U8, kind="ExternalInput").ap()
    k1h = nc.dram_tensor("k1", [FC, 128, FC * 128], U8, kind="ExternalInput").ap()
    k2h = nc.dram_tensor("k2", [CH, 128, FC * 128], U8, kind="ExternalInput").ap()
    sch = nc.dram_tensor("svals", [1, 3], F32, kind="ExternalInput").ap()
    cfh = nc.dram_tensor("cf", [128, NCF], F32, kind="ExternalInput").ap()
    onbh = nc.dram_tensor("onb", [128, 128], BF16, kind="ExternalInput").ap()
    yh = nc.dram_tensor("y", [D, T], F32, kind="ExternalOutput").ap()

    es = ExitStack()
    mk = es.enter_context
    xln = mk(nc.sbuf_tensor("xln", [128, CH * T], BF16))
    ngx = mk(nc.sbuf_tensor("ngx", [128, CH * T], BF16))
    g1b = mk(nc.sbuf_tensor("g1b", [128, FC * 512], BF16))
    g2b = mk(nc.sbuf_tensor("g2b", [128, FC * 512], BF16))
    kab = mk(nc.sbuf_tensor("kab", [128, 2 * 2 * 1024], BF16))
    kbuf = mk(nc.sbuf_tensor("kbuf", [128, 2 * FC * 128], BF16))
    xsl = mk(nc.sbuf_tensor("xsl", [128, CH * 512], F32))
    lnr = mk(nc.sbuf_tensor("lnr", [1, 4 * 512], F32))
    sq = mk(nc.sbuf_tensor("sq", [128, 512], F32))
    t3 = mk(nc.sbuf_tensor("t3", [128, 512], F32))
    rbc = mk(nc.sbuf_tensor("rbc", [128, 512], F32))
    r_row = mk(nc.sbuf_tensor("r_row", [1, 512], F32))
    scol = mk(nc.sbuf_tensor("scol", [128, 5], F32))
    onb = mk(nc.sbuf_tensor("onb_s", [128, 128], BF16))
    srow = mk(nc.sbuf_tensor("srow", [1, 3], F32))
    cf = mk(nc.sbuf_tensor("cf_s", [128, NCF], F32))

    p0 = mk(nc.psum_tensor("p0", [128, 512], F32))
    p1 = mk(nc.psum_tensor("p1", [128, 512], F32))
    po = mk(nc.psum_tensor("po", [128, 512], F32))
    pt = mk(nc.psum_tensor("pt", [128, 512], F32))

    seq = Seq(nc, es)
    blk = mk(nc.Block())
    SLD, SST = seq.sem["ld"], seq.sem["st"]
    SPE, SDVE, SACT = seq.sem["pe"], seq.sem["dve"], seq.sem["act"]

    ONESC = cf[:, _ONES:_ONES + 1]
    ONESR = cf[0:1, _ONES:_ONES + 128]

    def ld(g, dst, src):
        g.dma_start(out=dst, in_=src).then_inc(SLD, 16)
        seq.cnt["ld"] += 16

    def st(s, dst, src):
        s.dma_start(out=dst, in_=src).then_inc(SST, 16)
        seq.cnt["st"] += 16

    @blk.gpsimd
    def _(g):
        ld(g, cf[:], cfh)
        ld(g, srow[:], sch)
        ld(g, onb[:], onbh)
    lds0 = seq.cnt["ld"]

    @blk.tensor
    def _(t):
        t.wait_ge(SLD, lds0)
        for j in range(3):
            m = t.matmul(pt[:, j:j + 1], ONESR, srow[0:1, j:j + 1], start=True, stop=True)
        seq.mark("pe", m)
    pe_s = seq.cnt["pe"]

    @blk.scalar
    def _(s):
        s.wait_ge(SPE, pe_s)
        seq.mark("act", s.mul(out=scol[:, 0:1], in_=pt[:, 0:1], mul=2.0))
        seq.mark("act", s.mul(out=scol[:, 1:2], in_=pt[:, 1:2], mul=2.0))
        seq.mark("act", s.mul(out=scol[:, 2:3], in_=pt[:, 1:2], mul=4.0))
        seq.mark("act", s.mul(out=scol[:, 3:4], in_=pt[:, 2:3], mul=2.0))
        seq.mark("act", s.mul(out=scol[:, 4:5], in_=pt[:, 2:3], mul=4.0))
    a_scol = seq.cnt["act"]

    for sl in range(NTR):
        s0 = seq.snap()

        @blk.gpsimd
        def _(g):
            seq.wait(g, s0, only=("dve", "act", "pe"))
            for c in range(CH):
                ld(g, xsl[:, c * 512:(c + 1) * 512],
                   xh[c * 128:(c + 1) * 128, sl * 512:(sl + 1) * 512])
        lds = seq.cnt["ld"]

        pe0 = seq.cnt["pe"]
        a0 = seq.cnt["act"]

        @blk.scalar
        def _(s):
            s.wait_ge(SLD, lds)
            seq.wait(s, s0, only=("pe", "dve"))
            for c in range(CH):
                if c >= 1:
                    s.wait_ge(SPE, pe0 + c)
                seq.mark("act", s.square(out=sq[:], in_=xsl[:, c * 512:(c + 1) * 512]))

        @blk.tensor
        def _(t):
            t.wait_ge(SLD, lds)
            seq.wait(t, s0, only=("act", "dve"))
            for c in range(CH):
                t.matmul(po[0:1, :], ONESC, xsl[:, c * 512:(c + 1) * 512],
                         start=(c == 0), stop=(c == CH - 1))
                t.wait_ge(SACT, a0 + c + 1)
                seq.mark("pe", t.matmul(pt[0:1, :], ONESC, sq[:],
                                        start=(c == 0), stop=(c == CH - 1)))
        pe_st1 = seq.cnt["pe"]

        sM, sV, sR, sMR = slice(0, 512), slice(512, 1024), slice(1024, 1536), slice(1536, 2048)

        @blk.scalar
        def _(s):
            s.wait_ge(SPE, pe_st1)
            seq.mark("act", s.mul(out=lnr[0:1, sM], in_=po[0:1, :], mul=RD))
            s.wait_ge(SACT, seq.cnt["act"])
            seq.mark("act", s.mul(out=lnr[0:1, sV], in_=pt[0:1, :], mul=RD))
        a1 = seq.cnt["act"]

        @blk.vector
        def _(v):
            v.wait_ge(SACT, a1)
            seq.mark("dve", v.tensor_mul(lnr[0:1, sR], lnr[0:1, sM], lnr[0:1, sM]))
            v.wait_ge(SDVE, seq.cnt["dve"])
            seq.mark("dve", v.tensor_sub(lnr[0:1, sV], lnr[0:1, sV], lnr[0:1, sR]))
        d1 = seq.cnt["dve"]

        @blk.scalar
        def _(s):
            s.wait_ge(SDVE, d1)
            seq.mark("act", s.activation(out=lnr[0:1, sR], in_=lnr[0:1, sV], func=AF.Ln,
                                         bias=cf[0:1, _EPSC:_EPSC + 1]))
            s.wait_ge(SACT, seq.cnt["act"])
            seq.mark("act", s.activation(out=lnr[0:1, sR], in_=lnr[0:1, sR],
                                         func=AF.Exp, scale=-0.5))
        a2 = seq.cnt["act"]

        @blk.vector
        def _(v):
            v.wait_ge(SACT, a2)
            seq.mark("dve", v.tensor_mul(lnr[0:1, sMR], lnr[0:1, sM], lnr[0:1, sR]))
        d2 = seq.cnt["dve"]

        @blk.tensor
        def _(t):
            t.wait_ge(SDVE, d2)
            t.matmul(p0[:], ONESR, lnr[0:1, sR], start=True, stop=True)
            seq.mark("pe", t.matmul(p1[:], ONESR, lnr[0:1, sMR], start=True, stop=True))
        pe_bc = seq.cnt["pe"]

        @blk.vector
        def _(v):
            v.wait_ge(SPE, pe_bc)
            for c in range(CH):
                cs = slice(c * T + sl * 512, c * T + (sl + 1) * 512)
                seq.mark("dve", v.tensor_mul(sq[:], xsl[:, c * 512:(c + 1) * 512], p0[:]))
                v.wait_ge(SDVE, seq.cnt["dve"])
                seq.mark("dve", v.tensor_sub(xln[:, cs], sq[:], p1[:]))
                v.wait_ge(SDVE, seq.cnt["dve"])
                seq.mark("dve", v.tensor_scalar_mul(ngx[:, cs], xln[:, cs], -1.0))
                v.wait_ge(SDVE, seq.cnt["dve"])
    ln_done = seq.snap()

    psum_hist = {}
    pidx = [0]

    def next_psum():
        i = pidx[0]
        pidx[0] += 1
        return (p0 if i % 2 == 0 else p1), i

    def psum_guard(t, i):
        if i >= 2:
            nm, cc = psum_hist[i - 2]
            t.wait_ge(seq.sem[nm], cc)

    slot_cons = {("ka", 0): 0, ("ka", 1): 0, ("kb", 0): 0, ("kb", 1): 0}
    d_rbc_user = 0
    first = [True]

    for tr in range(NTR):
        for oc in range(FC):
            slot = oc % 2

            @blk.gpsimd
            def _(g):
                seq.wait_at(g, "pe", slot_cons[("ka", slot)])
                ld(g, kab[:, slot * 2048:slot * 2048 + 1024], k0ah[oc])
                ld(g, kab[:, slot * 2048 + 1024:slot * 2048 + 2048], k0bh[oc])
            kld = seq.cnt["ld"]

            ps, pi = next_psum()

            @blk.tensor
            def _(t):
                t.wait_ge(SLD, kld)
                if first[0]:
                    seq.wait(t, ln_done, only=("dve",))
                    t.wait_ge(SACT, a_scol)
                    first[0] = False
                psum_guard(t, pi)
                for c in range(CH):
                    t.matmul(ps[:], kab[:, slot * 2048 + c * 128:slot * 2048 + (c + 1) * 128],
                             xln[:, c * T + tr * 512:c * T + (tr + 1) * 512],
                             start=(c == 0), stop=False)
                for c in range(CH):
                    m = t.matmul(ps[:], kab[:, slot * 2048 + 1024 + c * 128:slot * 2048 + 1024 + (c + 1) * 128],
                                 ngx[:, c * T + tr * 512:c * T + (tr + 1) * 512],
                                 start=False, stop=(c == CH - 1))
                seq.mark("pe", m)
            pe_c0 = seq.cnt["pe"]
            slot_cons[("ka", slot)] = pe_c0

            @blk.scalar
            def _(s):
                s.wait_ge(SPE, pe_c0)
                seq.mark("act", s.activation(out=g1b[:, oc * 512:(oc + 1) * 512],
                                             in_=ps[:], func=AF.Relu,
                                             scale=scol[:, 0:1]))
            psum_hist[pi] = ("act", seq.cnt["act"])
        a_g1 = seq.cnt["act"]

        @blk.tensor
        def _(t):
            t.wait_ge(SACT, a_g1)
            t.wait_ge(SDVE, d_rbc_user)
            for fc in range(FC):
                m = t.matmul(po[0:1, :], onb[:, 0:1],
                             g1b[:, fc * 512:(fc + 1) * 512],
                             start=(fc == 0), stop=(fc == FC - 1))
            seq.mark("pe", m)
        pe_r1 = seq.cnt["pe"]

        @blk.vector
        def _(v):
            v.wait_ge(SPE, pe_r1)
            seq.mark("dve", v.tensor_copy(out=r_row[:], in_=po[0:1, :]))
        d_r1 = seq.cnt["dve"]

        @blk.tensor
        def _(t):
            t.wait_ge(SDVE, d_r1)
            seq.mark("pe", t.matmul(pt[:], ONESR, r_row[:], start=True, stop=True))
        pe_r1b = seq.cnt["pe"]

        @blk.vector
        def _(v):
            v.wait_ge(SPE, pe_r1b)
            seq.mark("dve", v.tensor_scalar_mul(rbc[:], pt[:], scol[:, 1:2]))
            v.wait_ge(SDVE, seq.cnt["dve"])
        d_rbc = seq.cnt["dve"]

        for oc in range(FC):
            slot = oc % 2

            @blk.gpsimd
            def _(g):
                seq.wait_at(g, "pe", slot_cons[("kb", slot)])
                ld(g, kbuf[:, slot * FC * 128:(slot + 1) * FC * 128], k1h[oc])
            kld = seq.cnt["ld"]

            ps, pi = next_psum()

            @blk.tensor
            def _(t):
                t.wait_ge(SLD, kld)
                psum_guard(t, pi)
                for fc in range(FC):
                    m = t.matmul(ps[:], kbuf[:, slot * FC * 128 + fc * 128:slot * FC * 128 + (fc + 1) * 128],
                                 g1b[:, fc * 512:(fc + 1) * 512],
                                 start=(fc == 0), stop=(fc == FC - 1))
                seq.mark("pe", m)
            pe_c1 = seq.cnt["pe"]
            slot_cons[("kb", slot)] = pe_c1

            @blk.vector
            def _(v):
                v.wait_ge(SPE, pe_c1)
                if oc == 0:
                    v.wait_ge(SDVE, d_rbc)
                seq.mark("dve", v.scalar_tensor_tensor(
                    out=t3[:], in0=ps[:], scalar=scol[:, 2:3], in1=rbc[:],
                    op0=ALU.mult, op1=ALU.subtract))
                v.wait_ge(SDVE, seq.cnt["dve"])
            d_t3 = seq.cnt["dve"]

            @blk.scalar
            def _(s):
                s.wait_ge(SDVE, d_t3)
                seq.mark("act", s.activation(out=g2b[:, oc * 512:(oc + 1) * 512],
                                             in_=t3[:], func=AF.Relu))
            psum_hist[pi] = ("act", seq.cnt["act"])
        a_g2 = seq.cnt["act"]

        @blk.tensor
        def _(t):
            t.wait_ge(SACT, a_g2)
            for fc in range(FC):
                m = t.matmul(po[0:1, :], onb[:, 0:1],
                             g2b[:, fc * 512:(fc + 1) * 512],
                             start=(fc == 0), stop=(fc == FC - 1))
            seq.mark("pe", m)
        pe_r2 = seq.cnt["pe"]

        @blk.vector
        def _(v):
            v.wait_ge(SPE, pe_r2)
            seq.mark("dve", v.tensor_copy(out=r_row[:], in_=po[0:1, :]))
        d_r2 = seq.cnt["dve"]

        @blk.tensor
        def _(t):
            t.wait_ge(SDVE, d_r2)
            seq.mark("pe", t.matmul(pt[:], ONESR, r_row[:], start=True, stop=True))
        pe_r2b = seq.cnt["pe"]

        @blk.vector
        def _(v):
            v.wait_ge(SPE, pe_r2b)
            seq.mark("dve", v.tensor_scalar_mul(rbc[:], pt[:], scol[:, 3:4]))
            v.wait_ge(SDVE, seq.cnt["dve"])
        d_rbc2 = seq.cnt["dve"]

        xld = []
        s0 = seq.snap()

        @blk.gpsimd
        def _(g):
            seq.wait(g, s0, only=("dve", "st"))
            for oc in range(CH):
                ld(g, xsl[:, oc * 512:(oc + 1) * 512],
                   xh[oc * 128:(oc + 1) * 128, tr * 512:(tr + 1) * 512])
                xld.append(seq.cnt["ld"])
        xld_all = xld[-1]

        for oc in range(CH):
            slot = oc % 2

            @blk.gpsimd
            def _(g):
                seq.wait_at(g, "pe", slot_cons[("kb", slot)])
                ld(g, kbuf[:, slot * FC * 128:(slot + 1) * FC * 128], k2h[oc])
            kld = seq.cnt["ld"]

            ps, pi = next_psum()

            @blk.tensor
            def _(t):
                t.wait_ge(SLD, kld)
                psum_guard(t, pi)
                for fc in range(FC):
                    m = t.matmul(ps[:], kbuf[:, slot * FC * 128 + fc * 128:slot * FC * 128 + (fc + 1) * 128],
                                 g2b[:, fc * 512:(fc + 1) * 512],
                                 start=(fc == 0), stop=(fc == FC - 1))
                seq.mark("pe", m)
            pe_c2 = seq.cnt["pe"]
            slot_cons[("kb", slot)] = pe_c2

            @blk.vector
            def _(v):
                v.wait_ge(SPE, pe_c2)
                if oc == 0:
                    v.wait_ge(SDVE, d_rbc2)
                v.wait_ge(SLD, xld_all)
                seq.mark("dve", v.scalar_tensor_tensor(
                    out=t3[:], in0=ps[:], scalar=scol[:, 4:5], in1=rbc[:],
                    op0=ALU.mult, op1=ALU.subtract))
                v.wait_ge(SDVE, seq.cnt["dve"])
                seq.mark("dve", v.tensor_add(xsl[:, oc * 512:(oc + 1) * 512],
                                             xsl[:, oc * 512:(oc + 1) * 512], t3[:]))
                v.wait_ge(SDVE, seq.cnt["dve"])
            psum_hist[pi] = ("dve", seq.cnt["dve"])
            d_y = seq.cnt["dve"]

            @blk.sync
            def _(s):
                s.wait_ge(SDVE, d_y)
                st(s, yh[oc * 128:(oc + 1) * 128, tr * 512:(tr + 1) * 512],
                   xsl[:, oc * 512:(oc + 1) * 512])
        d_rbc_user = seq.cnt["dve"]

    @blk.sync
    def _(s):
        s.wait_ge(SST, seq.cnt["st"])

    es.close()
    return nc


def _prep_ld_shared(inp):
    cfc, _, _ = host_consts()
    k0 = np.ascontiguousarray(inp["L1_ffn_k0"]).view(np.uint8)
    k1 = np.ascontiguousarray(inp["L1_ffn_k1"]).view(np.uint8)
    k2 = np.ascontiguousarray(inp["L1_ffn_k2"]).view(np.uint8)
    k0a = np.ascontiguousarray(
        k0[:D].reshape(CH, 128, FC, 128).transpose(2, 1, 0, 3).reshape(FC, 128, CH * 128))
    k0b = np.ascontiguousarray(
        k0[D:].reshape(CH, 128, FC, 128).transpose(2, 1, 0, 3).reshape(FC, 128, CH * 128))
    k1t = np.ascontiguousarray(
        k1.reshape(FC, 128, FC, 128).transpose(2, 1, 0, 3).reshape(FC, 128, FC * 128))
    k2t = np.ascontiguousarray(
        k2.reshape(FC, 128, CH, 128).transpose(2, 1, 0, 3).reshape(CH, 128, FC * 128))
    sv = np.array([[inp["L1_ffn_s0"], inp["L1_ffn_s1"], inp["L1_ffn_s2"]]], np.float32)
    return cfc, k0a, k0b, k1t, k2t, sv


def _prep_ld_inputs(inp, x3_list):
    cfc, k0a, k0b, k1t, k2t, sv = _prep_ld_shared(inp)
    maps = []
    for b in range(8):
        maps.append({
            "x3": x3_list[b],
            "k0a": k0a, "k0b": k0b, "k1": k1t, "k2": k2t,
            "svals": sv, "cf": cfc, "onb": host_consts()[2],
        })
    return maps


def kernel(**inputs):
    inp = inputs
    ra = run_bass_kernel_spmd(build_la(), _prep_la_inputs(inp),
                              core_ids=list(range(8))).results
    rb = run_bass_kernel_spmd(build_lb(), _prep_lb_inputs(inp, ra),
                              core_ids=list(range(8))).results
    x2_list = [rb[b]["x2"] for b in range(8)]
    rc = run_bass_kernel_spmd(build_lc(), _prep_lc_inputs(inp, x2_list),
                              core_ids=list(range(8))).results
    x3_list = [rc[b]["x3"] for b in range(8)]
    rd = run_bass_kernel_spmd(build_ld(), _prep_ld_inputs(inp, x3_list),
                              core_ids=list(range(8))).results
    return np.stack([rd[b]["y"].T for b in range(8)])
